# version 04 (final)
# baseline (speedup 1.0000x reference)
import numpy as np
import concourse.bass as bass
import concourse.mybir as mybir
from concourse.bass_utils import run_bass_kernel_spmd
from concourse.expressions_rust import RuntimeValue
from concourse.bass_core_rust import IndirectOffsetOnAxis

F32 = mybir.dt.float32
BF16 = mybir.dt.bfloat16
U32 = mybir.dt.uint32
AF = mybir.ActivationFunctionType
ALU = mybir.AluOpType
AX = mybir.AxisListType

B, T, D = 4, 4096, 2048
NTOK = 2048
H, DK, DV = 16, 128, 128
NCH = 16
ZROWS = 2176
LN_EPS = 1e-5

DEBUG_TAPS = ()
DBG_AT = (0, 0, 0)


CONST_VALUES = [1e-5, -0.5, 1e-6, 1e-8, 0.5, 1.0, 2.0, -1.0, -2.0,
                0.0883883476483184, 0.2402265069591007, 3.141592653589793,
                -3.141592653589793, 6.283185307179586, 1.5707963267948966]


class Ctx:
    def __init__(self, nc):
        self.nc = nc
        self.block = None
        self.const_idx = {v: i for i, v in enumerate(CONST_VALUES)}
        self.sem = {}
        self.cnt = {}
        self.sb = {}
        self.ps = {}
        self.taps = {}
        self.alloc_stack = []

    def S(self, name):
        if name not in self.sem:
            self.sem[name] = self.nc.semaphore(name).__enter__()
            self.cnt[name] = 0
        return self.sem[name]

    def inc(self, name, instr, k=1):
        instr.then_inc(self.S(name), k)
        self.cnt[name] += k
        return self.cnt[name]

    def w(self, eng, name, val=None):
        eng.wait_ge(self.S(name), self.cnt[name] if val is None else val)

    def sbuf(self, name, shape, dtype):
        g = self.nc.sbuf_tensor(name, shape, dtype)
        h = g.__enter__()
        self.sb[name] = h
        self.alloc_stack.append((name, g))
        return h

    def psum(self, name, shape, dtype=F32):
        g = self.nc.psum_tensor(name, shape, dtype)
        h = g.__enter__()
        self.ps[name] = h
        self.alloc_stack.append((name, g))
        return h

    def cc(self, value):
        if value not in self.const_idx:
            raise KeyError(f"const {value} not pre-declared in CONST_VALUES")
        i = self.const_idx[value]
        return self.sb["constpool"][:, i:i + 1]

    def mark(self):
        return len(self.alloc_stack)

    def release(self, mark):
        while len(self.alloc_stack) > mark:
            name, g = self.alloc_stack.pop()
            g.__exit__(None, None, None)
            self.sb.pop(name, None)
            self.ps.pop(name, None)


def ceil_div(a, b):
    return (a + b - 1) // b



def build_nc():
    nc = bass.Bass("TRN2", target_bir_lowering=False, debug=False)
    c = Ctx(nc)
    I = {}

    def inp(name, shape):
        I[name] = nc.dram_tensor(name, shape, F32, kind="ExternalInput").ap()

    inp("hs", [NTOK, D])
    inp("hs_bnd", [3, D])
    for n in ["wq", "wk", "wv"]:
        inp(n, [D, D])
    for n in ["conv_q_w", "conv_k_w", "conv_v_w"]:
        inp(n, [D, 4])
    inp("A_log", [16]); inp("dt_bias", [D])
    inp("proxy_w", [128, 128]); inp("unc_w", [D, 64]); inp("head_embed", [16, 4])
    inp("mlp_w1", [8, 32]); inp("mlp_b1", [32]); inp("mlp_w2", [32, 32]); inp("mlp_b2", [32])
    for n in ["bb_w", "bd_w", "lam_w", "ab_w", "ad_w"]:
        inp(n, [32])
    for n in ["bb_b", "bd_b", "lam_b", "ab_b", "ad_b"]:
        inp(n, [1])
    inp("g1_w", [D, 128]); inp("g2_w", [128, D]); inp("g2_b", [D]); inp("onorm_w", [128])
    inp("wo", [D, D])
    inp("b_Wq_down", [512, D]); inp("b_bq_down", [512])
    inp("b_Wkv_down", [512, D]); inp("b_bkv_down", [512])
    inp("b_Wq_up", [1024, 512]); inp("b_bq_up", [1024])
    inp("b_Wk_up", [1024, 512]); inp("b_bk_up", [1024])
    inp("b_Wv_up", [D, 512]); inp("b_bv_up", [D])
    inp("b_Wo", [D, D]); inp("b_bo", [D])
    inp("f_sw1", [2, D, 1408]); inp("f_sb1", [2, 1408])
    inp("f_sw2", [2, 1408, D]); inp("f_sb2", [2, D])
    inp("f_rw1", [16, D, 1408]); inp("f_rb1", [16, 1408])
    inp("f_rw2", [16, 1408, D]); inp("f_rb2", [16, D])
    inp("f_cent", [16, D]); inp("f_bias", [16])
    inp("bigmask", [128, 128]); inp("mask4", [128, 512])
    I["kqv_idx"] = nc.dram_tensor("kqv_idx", [8, 256], mybir.dt.int32, kind="ExternalInput").ap()
    I["srow_idx"] = nc.dram_tensor("srow_idx", [1, 80], mybir.dt.int32, kind="ExternalInput").ap()
    I["gch_idx"] = nc.dram_tensor("gch_idx", [1, 8], mybir.dt.int32, kind="ExternalInput").ap()
    I["amp_idx"] = nc.dram_tensor("amp_idx", [1, 32], mybir.dt.int32, kind="ExternalInput").ap()
    I["tail_idx"] = nc.dram_tensor("tail_idx", [256, 128], mybir.dt.int32, kind="ExternalInput").ap()
    I["pos"] = nc.dram_tensor("pos", [NTOK], F32, kind="ExternalInput").ap()
    I["invfreq"] = nc.dram_tensor("invfreq", [256], F32, kind="ExternalInput").ap()

    out = nc.dram_tensor("out", [NTOK, D], F32, kind="ExternalOutput").ap()

    DR = {}
    def dram(name, shape, dtype):
        DR[name] = nc.dram_tensor(name, shape, dtype, kind="Internal").ap()
        return DR[name]

    dram("zstore", [ZROWS, D], BF16)
    dram("qf", [D, NTOK], BF16)
    dram("kf", [D, NTOK], BF16)
    dram("vf", [D, NTOK], BF16)
    dram("qg", [2 * D, NTOK], BF16)
    dram("kg", [2 * D, NTOK], BF16)
    dram("vg", [2 * D, NTOK], BF16)
    dram("scal_row", [80, NTOK], F32)
    dram("scal_rowg", [160, NTOK], F32)
    dram("scal_tok", [NTOK, 80], F32)
    dram("scal_tokg", [2 * NTOK, 80], F32)
    dram("gsig", [NTOK, D], BF16)
    dram("ostore", [8 * T, DV], F32)
    dram("og", [2 * 8 * T, DV], F32)
    dram("x1s", [NTOK, D], F32)
    dram("x2s", [NTOK, D], F32)
    dram("x3s", [NTOK, D], F32)
    dram("z2s", [NTOK, D], BF16)
    dram("hstore", [18 * 11 * 128, D], BF16)
    dram("ystore", [NTOK, D], F32)
    dram("wqd_b", [512, D], BF16)
    dram("wkvd_b", [512, D], BF16)
    dram("wqu_b", [1024, 512], BF16)
    dram("wku_b", [1024, 512], BF16)
    dram("wvu_b", [D, 512], BF16)
    dram("wo_b", [D, D], BF16)
    dram("qb_d", [NTOK, D], BF16)
    dram("kb_d", [NTOK, D], BF16)
    dram("vb_d", [NTOK, D], BF16)
    dram("kbg", [2 * NTOK, D], BF16)
    dram("vbg", [2 * NTOK, D], BF16)
    dram("obd", [NTOK, D], BF16)
    dram("vhd", [128, T], BF16)
    dram("arow_d", [1, T], F32)
    dram("gmy_d", [8, 128], F32)
    dram("gchT_d", [16, 128], F32)
    dram("of_scr", [T, DV], F32)
    dram("usmall", [1, 160], F32)
    dram("bsmall", [1, 5], F32)

    taps = {}
    c.taps = taps
    def tap(name, shape, dtype=F32):
        if name in DEBUG_TAPS:
            taps[name] = nc.dram_tensor("tap_" + name, shape, dtype, kind="ExternalOutput").ap()
            return taps[name]
        return None

    t_z1 = tap("z1", [ZROWS, D], BF16)
    t_qf = tap("qf", [D, NTOK], BF16)
    t_kf = tap("kf", [D, NTOK], BF16)
    t_vf = tap("vf", [D, NTOK], BF16)
    t_scal = tap("scal", [80, NTOK])
    t_gsig = tap("gsig", [NTOK, D], BF16)
    t_o = tap("o", [8 * T, DV])
    for dn, dshape, ddt in [("dGA", [128, 260], F32), ("dB", [128, 128], BF16),
                            ("dP", [128, 128], BF16), ("du", [128, 128], BF16),
                            ("dosb", [128, 128], F32), ("dS", [128, 128], F32),
                            ("dku", [128, 256], BF16), ("dqs", [128, 256], BF16),
                            ("dbb", [128, 256], BF16), ("dvT", [128, 128], BF16),
                            ("dkbn", [128, 256], BF16), ("dkl8", [128, 256], BF16),
                            ("dkr8", [128, 256], BF16), ("dkl64", [128, 256], BF16),
                            ("dlcol", [128, 1], F32), ("doft", [128, 128], F32),
                            ("docmb", [128, 128], F32), ("dgcol", [128, 1], F32)]:
        tap(dn, dshape, ddt)
    t_x1 = tap("x1", [NTOK, D])
    t_x2 = tap("x2", [NTOK, D])
    t_x3 = tap("x3", [NTOK, D])

    block = nc.Block().__enter__()
    c.block = block

    emit_init(c, block)
    emit_ln(c, block, I["hs"], I["hs_bnd"], DR["zstore"], None, t_z1, first=True)
    emit_attn_a_proj(c, block, I, DR, t_qf, t_kf, t_vf, t_scal, t_gsig)
    emit_pair_gathers_a(c, block, DR)
    emit_scan(c, block, I, DR)
    emit_gather_o(c, block, DR, t_o)
    emit_tail_a(c, block, I, DR, t_x1)
    import os as _os
    _stage = int(_os.environ.get("AB_STAGE", "3"))
    if _stage >= 1:
        emit_ffn(c, block, I, DR, xin=DR["x1s"], xout=DR["x2s"], tap_out=t_x2, px="f1",
                 dep=["taW"])
    if _stage >= 2:
        emit_attn_b(c, block, I, DR, t_x3)
    if _stage >= 3:
        emit_ffn(c, block, I, DR, xin=DR["x3s"], xout=out, tap_out=None, final=True, px="f2",
                 dep=["abW"])

    block.__exit__(None, None, None)
    return nc, list(taps.keys())


def emit_init(c, block):
    nc = c.nc
    pool = c.sbuf("constpool", [128, len(CONST_VALUES)], F32)
    ident = c.sbuf("identbf", [128, 128], BF16)
    identf = c.sbuf("identf32", [128, 128], F32)
    onesb_row = c.sbuf("onesb_row", [1, 512], BF16)
    onesb_col = c.sbuf("onesb_col", [128, 1], BF16)
    onesf_row = c.sbuf("onesf_row", [1, 128], F32)

    @block.vector
    def _(v):
        for val, i in c.const_idx.items():
            c.inc("init", v.memset(pool[:, i:i + 1], float(val)))
        c.inc("init", v.memset(ident[:], 1.0))
        c.inc("init", v.memset(identf[:], 1.0))
        c.inc("init", v.memset(onesb_row[:], 1.0))
        c.inc("init", v.memset(onesb_col[:], 1.0))
        c.inc("init", v.memset(onesf_row[:], 1.0))

    @block.gpsimd
    def _(g):
        c.w(g, "init")
        c.inc("initp", g.affine_select(ident[:], ident[:], pattern=[[-1, 128]],
                                       compare_op=ALU.is_equal, fill=0.0, base=0,
                                       channel_multiplier=1))
        c.inc("initp", g.affine_select(identf[:], identf[:], pattern=[[-1, 128]],
                                       compare_op=ALU.is_equal, fill=0.0, base=0,
                                       channel_multiplier=1))

    for engf in [block.scalar, block.tensor, block.sync]:
        @engf
        def _(e):
            c.w(e, "init")
            c.w(e, "initp")



def emit_ln(c, block, xin, xbnd, zst, dep_sems, tp, first=False, prefix="ln1"):
    nc = c.nc
    mk = c.mark()
    xt = c.sbuf(prefix + "_xt", [128, D], F32)
    zt = c.sbuf(prefix + "_zt", [128, D], BF16)
    sums = c.sbuf(prefix + "_sums", [128, 1], F32)
    sumsq = c.sbuf(prefix + "_ssq", [128, 1], F32)
    scr = c.sbuf(prefix + "_scr", [128, D], F32)
    negmu = c.sbuf(prefix + "_negmu", [128, 1], F32)
    var = c.sbuf(prefix + "_var", [128, 1], F32)
    rstd = c.sbuf(prefix + "_rstd", [128, 1], F32)

    sL = prefix + "L"; sV = prefix + "V"; sS = prefix + "S"; sW = prefix + "W"
    deps = [(d, c.cnt[d]) for d in (dep_sems or [])]

    tiles = ([("b", 3)] if first else []) + [(i, 128) for i in range(NTOK // 128)]

    if first:
        zeros = c.sbuf(prefix + "_zeros", [128, D], BF16)
        @block.vector
        def _(v):
            c.inc(sV, v.memset(zeros[:], 0.0))
        @block.gpsimd
        def _(g):
            c.w(g, sV)
            c.inc(sW, g.dma_start(out=zst[0:125, :], in_=zeros[0:125, :]), 16)

    for idx, (ti, rows) in enumerate(tiles):
        @block.sync
        def _(s, ti=ti, rows=rows):
            for d, v in deps:
                c.w(s, d, v)
            c.w(s, sW)
            if ti == "b":
                c.inc(sL, s.dma_start(out=xt[0:3, :], in_=xbnd[:, :]), 16)
            else:
                c.inc(sL, s.dma_start(out=xt[0:rows, :], in_=xin[ti * 128:(ti + 1) * 128, :]), 16)
        @block.scalar
        def _(s, rows=rows):
            c.w(s, sL)
            c.inc(sS, s.activation(scr[0:rows, :], xt[0:rows, :], AF.Square,
                                   accum_out=sumsq[0:rows]))
        @block.vector
        def _(v, rows=rows):
            c.w(v, sL)
            c.inc(sV, v.tensor_reduce(sums[0:rows], xt[0:rows, :], AX.X, ALU.add))
            c.w(v, sV)
            c.inc(sV, v.tensor_scalar_mul(negmu[0:rows], sums[0:rows], -1.0 / D))
            c.w(v, sS)
            c.inc(sV, v.tensor_scalar_mul(var[0:rows], sumsq[0:rows], 1.0 / D))
            c.w(v, sV)
            c.inc(sV, v.tensor_tensor(scr[0:rows, 0:1], negmu[0:rows], negmu[0:rows], ALU.mult))
            c.w(v, sV)
            c.inc(sV, v.tensor_tensor(var[0:rows], var[0:rows], scr[0:rows, 0:1], ALU.subtract))
        @block.scalar
        def _(s, rows=rows):
            c.w(s, sV)
            c.inc(sS, s.activation(rstd[0:rows], var[0:rows], AF.Ln, bias=c.cc(LN_EPS)[0:rows]))
            c.w(s, sS)
            c.inc(sS, s.activation(rstd[0:rows], rstd[0:rows], AF.Exp, scale=c.cc(-0.5)[0:rows]))
        @block.vector
        def _(v, rows=rows):
            c.w(v, sS)
            c.inc(sV, v.tensor_scalar(zt[0:rows, :], xt[0:rows, :], negmu[0:rows],
                                      rstd[0:rows], ALU.add, ALU.mult))
        @block.gpsimd
        def _(g, ti=ti, rows=rows):
            c.w(g, sV)
            if ti == "b":
                dst = zst[125:128, :]
            else:
                r0 = (128 if first else 0) + ti * 128
                dst = zst[r0:r0 + rows, :]
            c.inc(sW, g.dma_start(out=dst, in_=zt[0:rows, :]), 16)
            if tp is not None:
                if ti == "b":
                    c.inc(sW, g.dma_start(out=tp[125:128, :], in_=zt[0:3, :]), 16)
                else:
                    r0 = (128 if first else 0) + ti * 128
                    c.inc(sW, g.dma_start(out=tp[r0:r0 + rows, :], in_=zt[0:rows, :]), 16)

    c.release(mk)


def emit_attn_a_proj(c, block, I, DR, t_qf, t_kf, t_vf, t_scal, t_gsig):
    nc = c.nc
    mk = c.mark()
    ident = c.sb["identbf"]
    ones_row = c.sb["onesb_row"]
    ones_col = c.sb["onesb_col"]

    z1T = [c.sbuf(f"z1T{i}", [128, ZROWS], BF16) for i in range(NCH)]
    wfull = [c.sbuf(f"wfull{i}", [128, D], BF16) for i in range(NCH)]
    convw = c.sbuf("convw", [128, 4], F32)
    co = c.sbuf("co", [128, NTOK], F32)
    po = c.sbuf("po", [128, NTOK], BF16)
    mk_pq = c.mark()
    pq = c.psum("pq", [128, ZROWS], F32)

    sZ = "apZ"
    sWl = "apWl"
    sT = "apT"
    sV = "apV"
    sS = "apS"
    sW = "apW"
    sC = "apC"

    @block.scalar
    def _(s):
        c.w(s, "ln1W")
        for i in range(NCH):
            c.inc(sZ, s.dma_start(out=z1T[i][:], in_=DR["zstore"][:, i * 128:(i + 1) * 128],
                                  transpose=True), 16)

    for pi, (wname, cname, dst, tp) in enumerate([
            ("wq", "conv_q_w", DR["qf"], t_qf),
            ("wk", "conv_k_w", DR["kf"], t_kf),
            ("wv", "conv_v_w", DR["vf"], t_vf)]):
        @block.gpsimd
        def _(g, wname=wname, pi=pi):
            if pi > 0:
                c.w(g, sW)
            for ch in range(NCH):
                c.inc(sWl, g.dma_start(out=wfull[ch][:], in_=I[wname][ch * 128:(ch + 1) * 128, :]), 16)
        for m in range(NCH):
            @block.sync
            def _(s, m=m, cname=cname):
                c.inc(sC, s.dma_start(out=convw[:], in_=I[cname][m * 128:(m + 1) * 128, :]), 16)
            @block.tensor
            def _(t, m=m):
                c.w(t, sZ)
                c.w(t, sWl)
                c.w(t, sV)
                for ch in range(NCH):
                    for nt in range(5):
                        n0 = nt * 512
                        n1 = min(ZROWS, n0 + 512)
                        ins = t.matmul(pq[:, n0:n1], wfull[ch][:, m * 128:(m + 1) * 128],
                                       z1T[ch][:, n0:n1], start=(ch == 0), stop=(ch == NCH - 1))
                c.inc(sT, ins)
            @block.vector
            def _(v, m=m):
                c.w(v, sT)
                c.w(v, sC)
                c.w(v, sS)
                c.inc(sV, v.tensor_scalar_mul(co[:], pq[:, 125:125 + NTOK], convw[:, 0:1]))
                for j in range(1, 4):
                    c.w(v, sV)
                    c.inc(sV, v.scalar_tensor_tensor(co[:], pq[:, 125 + j:125 + j + NTOK],
                                                     convw[:, j:j + 1], co[:], ALU.mult, ALU.add))
            @block.scalar
            def _(s, m=m):
                c.w(s, sV)
                c.w(s, sW)
                c.inc(sS, s.activation(po[:], co[:], AF.Silu))
            @block.sync
            def _(s, m=m, dst=dst, tp=tp):
                c.w(s, sS)
                c.inc(sW, s.dma_start(out=dst[m * 128:(m + 1) * 128, :], in_=po[:]), 16)
                if tp is not None:
                    c.inc(sW, s.dma_start(out=tp[m * 128:(m + 1) * 128, :], in_=po[:]), 16)

    @block.tensor
    def _(t):
        c.w(t, "apV")
    c.release(mk_pq)
    mk_pg = c.mark()
    g1o = c.sbuf("g1o", [128, NTOK], BF16)
    g2w = c.sbuf("g2w", [128, D], BF16)
    g2b = c.sbuf("g2b", [1, D], BF16)
    gsig_t = c.sbuf("gsig_t", [128, D], BF16)
    pg = c.psum("pg", [128, D], F32)

    @block.gpsimd
    def _(g):
        c.w(g, sW)
        for ch in range(NCH):
            c.inc("apWg", g.dma_start(out=wfull[ch][:, 0:128], in_=I["g1_w"][ch * 128:(ch + 1) * 128, :]), 16)
        c.inc("apWg", g.dma_start(out=g2w[:], in_=I["g2_w"][:, :]), 16)
        c.inc("apWg", g.dma_start(out=g2b[:], in_=I["g2_b"][None, :]), 16)
    @block.tensor
    def _(t):
        c.w(t, "apWg")
        for ch in range(NCH):
            for nt in range(4):
                n0 = 128 + nt * 512
                ins = t.matmul(pg[:, nt * 512:(nt + 1) * 512], wfull[ch][:, 0:128],
                               z1T[ch][:, n0:n0 + 512], start=(ch == 0), stop=(ch == NCH - 1))
        c.inc(sT, ins)
    @block.scalar
    def _(s):
        c.w(s, sT)
        c.inc(sS, s.copy(g1o[:], pg[:, 0:NTOK]))
    for tt in range(NCH):
        @block.tensor
        def _(t, tt=tt):
            c.w(t, sS)
            for nt in range(4):
                ins = t.matmul(pg[:, nt * 512:(nt + 1) * 512], g1o[:, tt * 128:(tt + 1) * 128],
                               g2w[:, nt * 512:(nt + 1) * 512], start=True, stop=False)
                ins = t.matmul(pg[:, nt * 512:(nt + 1) * 512], ones_row[0:1, 0:128],
                               g2b[0:1, nt * 512:(nt + 1) * 512], start=False, stop=True)
            c.inc(sT, ins)
        @block.scalar
        def _(s, tt=tt):
            c.w(s, sT)
            c.w(s, sW)
            c.inc(sS, s.activation(gsig_t[:], pg[:], AF.Sigmoid))
        @block.sync
        def _(s, tt=tt):
            c.w(s, sS)
            c.inc(sW, s.dma_start(out=DR["gsig"][tt * 128:(tt + 1) * 128, :], in_=gsig_t[:]), 16)
            if t_gsig is not None:
                c.inc(sW, s.dma_start(out=t_gsig[tt * 128:(tt + 1) * 128, :], in_=gsig_t[:]), 16)

    @block.tensor
    def _(t):
        c.w(t, "apS")
    c.release(mk_pg)
    entcol = c.sbuf("entcol", [128, NCH], F32)
    mk_pu = c.mark()
    uncw = [c.sbuf(f"uncw{i}", [128, 64], BF16) for i in range(NCH)]
    probs = c.sbuf("probs", [128, 64], F32)
    escr = c.sbuf("escr", [128, 64], F32)
    negm = c.sbuf("negm", [128, 1], F32)
    sume = c.sbuf("sume", [128, 1], F32)
    lnse = c.sbuf("lnse", [128, 1], F32)
    invse = c.sbuf("invse", [128, 1], F32)
    s1c = c.sbuf("s1c", [128, 1], F32)
    pu = c.psum("pu", [128, 64], F32)

    @block.gpsimd
    def _(g):
        c.w(g, "apS")
        c.w(g, "apT")
        c.w(g, "apW")
        for ch in range(NCH):
            c.inc("apWu", g.dma_start(out=uncw[ch][:], in_=I["unc_w"][ch * 128:(ch + 1) * 128, :]), 16)
    for tt in range(NCH):
        @block.tensor
        def _(t, tt=tt):
            c.w(t, "apWu")
            c.w(t, sV)
            for ch in range(NCH):
                ins = t.matmul(pu[:], z1T[ch][:, 128 + tt * 128:128 + (tt + 1) * 128],
                               uncw[ch][:], start=(ch == 0), stop=(ch == NCH - 1))
            c.inc(sT, ins)
        @block.vector
        def _(v, tt=tt):
            c.w(v, sT)
            c.inc(sV, v.tensor_reduce(negm[:], pu[:], AX.X, ALU.max, negate=True))
        @block.scalar
        def _(s, tt=tt):
            c.w(s, sV)
            c.inc(sS, s.activation(probs[:], pu[:], AF.Exp, bias=negm[:], accum_out=sume[:]))
            c.w(s, sS)
            c.inc(sS, s.activation(lnse[:], sume[:], AF.Ln))
        @block.vector
        def _(v, tt=tt):
            c.w(v, sS)
            c.inc(sV, v.tensor_tensor(escr[:], probs[:], pu[:], ALU.mult))
            c.w(v, sV)
            c.inc(sV, v.tensor_reduce(s1c[:], escr[:], AX.X, ALU.add))
            c.inc(sV, v.reciprocal(invse[:], sume[:]))
            c.w(v, sV)
            c.inc(sV, v.tensor_tensor(s1c[:], s1c[:], invse[:], ALU.mult))
            c.w(v, sV)
            c.inc(sV, v.tensor_tensor(lnse[:], lnse[:], negm[:], ALU.subtract))
            c.w(v, sV)
            c.inc(sV, v.tensor_tensor(lnse[:], lnse[:], s1c[:], ALU.subtract))
            c.w(v, sV)
            c.inc(sV, v.tensor_scalar_mul(entcol[:, tt:tt + 1], lnse[:], 1.0 / np.log(64.0)))

    @block.tensor
    def _(t):
        c.w(t, "apV")
        c.w(t, "apS")
    c.release(mk_pu)
    emit_feats(c, block, I, DR, z1T, entcol, t_scal)
    c.release(mk)


def emit_feats(c, block, I, DR, z1T, entcol, t_scal):
    nc = c.nc
    ones_row = c.sb["onesb_row"]
    ones_col = c.sb["onesb_col"]
    ident = c.sb["identbf"]

    proxy = c.sbuf("proxy", [128, 128], BF16)
    kf_h = c.sbuf("kf_h", [128, NTOK], BF16)
    vf_h = c.sbuf("vf_h", [128, NTOK], BF16)
    vhat = c.sbuf("vhat", [128, NTOK], BF16)
    sqv = c.sbuf("sqv", [128, NTOK], BF16)
    sqh = c.sbuf("sqh", [128, NTOK], BF16)
    pvh = c.sbuf("pvh", [128, NTOK], BF16)
    pstat = c.psum("pstat", [128, NCH * 48], F32)
    pvh_ps = c.psum("pvh_ps", [128, NTOK // 2], F32)

    sL = "ftL"; sT = "ftT"; sV = "ftV"; sS = "ftS"; sW = "ftW"

    @block.gpsimd
    def _(g):
        c.w(g, "apWu")
        c.w(g, "apV")
        c.w(g, "apS")
        c.w(g, "apT")
        c.inc(sL, g.dma_start(out=proxy[:], in_=I["proxy_w"][:, :]), 16)

    for h in range(H):
        @block.gpsimd
        def _(g, h=h):
            c.w(g, "apW")
            c.w(g, sS)
            c.w(g, sV)
            c.inc(sL, g.dma_start(out=kf_h[:], in_=DR["kf"][h * 128:(h + 1) * 128, :]), 16)
            c.inc(sL, g.dma_start(out=vf_h[:], in_=DR["vf"][h * 128:(h + 1) * 128, :]), 16)
        for half in range(2):
            @block.tensor
            def _(t, h=h, half=half):
                c.w(t, sL)
                c.w(t, sS)
                t.matmul(pvh_ps[:, 0:512], proxy[:], kf_h[:, half * 1024:half * 1024 + 512],
                         start=True, stop=True)
                ins = t.matmul(pvh_ps[:, 512:1024], proxy[:],
                               kf_h[:, half * 1024 + 512:half * 1024 + 1024],
                               start=True, stop=True)
                c.inc(sT, ins)
            @block.scalar
            def _(s, h=h, half=half):
                c.w(s, sT)
                c.inc(sS, s.activation(vhat[:, half * 1024:(half + 1) * 1024], pvh_ps[:, 0:1024],
                                       AF.Copy))
        @block.scalar
        def _(s, h=h):
            c.w(s, sS)
            c.inc(sS, s.activation(sqv[:], vf_h[:], AF.Square))
            c.w(s, sS)
            c.inc(sS, s.activation(sqh[:], vhat[:], AF.Square))
        @block.vector
        def _(v, h=h):
            c.w(v, sS)
            c.inc(sV, v.tensor_tensor(pvh[:], vf_h[:], vhat[:], ALU.mult))
        @block.tensor
        def _(t, h=h):
            c.w(t, sS)
            c.w(t, sV)
            for tt in range(NCH):
                base = tt * 48 + h * 3
                t.matmul(pstat[:, base:base + 1], sqv[:, tt * 128:(tt + 1) * 128], ones_col[:],
                         start=True, stop=True)
                t.matmul(pstat[:, base + 1:base + 2], pvh[:, tt * 128:(tt + 1) * 128], ones_col[:],
                         start=True, stop=True)
                ins = t.matmul(pstat[:, base + 2:base + 3], sqh[:, tt * 128:(tt + 1) * 128], ones_col[:],
                               start=True, stop=True)
            c.inc(sT, ins)

    emit_feats_mlp(c, block, I, DR, entcol, pstat, t_scal, sT)


def emit_feats_mlp(c, block, I, DR, entcol, pstat, t_scal, sT_stats):
    nc = c.nc
    ident = c.sb["identbf"]
    sL = "fmL"; sT = "fmT"; sV = "fmV"; sS = "fmS"; sW = "fmW"; sP = "fmP"

    BD1 = c.sbuf("BD1", [128, 512], BF16)
    bd2 = c.sbuf("bd2", [128, 128], BF16)
    W5 = [c.sbuf(f"W5_{i}", [128, 80], BF16) for i in range(4)]
    bias80 = c.sbuf("bias80", [80, 1], F32)
    b1col = c.sbuf("b1col", [128, 1], F32)
    b2col = c.sbuf("b2col", [128, 1], F32)
    he_row = c.sbuf("he_row", [1, 128], BF16)
    he_b = c.sbuf("he_b", [128, 128], BF16)
    uvec = c.sbuf("uvec", [1, 8 * 32], F32)
    urow = c.sbuf("urow", [1, 5 * 32], F32)
    bscal = c.sbuf("bscal", [1, 8], F32)
    bcomb = c.sbuf("bcomb", [1, 5], F32)
    stats_sb = c.sbuf("stats_sb", [128, 48], F32)
    errsq = c.sbuf("errsq", [128, 16], F32)
    vn = c.sbuf("vn", [128, 16], F32)
    hn = c.sbuf("hn", [128, 16], F32)
    en = c.sbuf("en", [128, 16], F32)
    r1 = c.sbuf("r1", [128, 16], F32)
    d2 = c.sbuf("d2", [128, 16], F32)
    feats = c.sbuf("feats", [128, 128], BF16)
    featsT = c.sbuf("featsT", [128, 128], BF16)
    hfT = c.sbuf("hfT", [128, 512], BF16)
    hf2T = c.sbuf("hf2T", [128, 512], BF16)
    sig = c.sbuf("sig", [80, 128], F32)
    sigT = c.sbuf("sigT", [128, 80], F32)
    pfe = c.psum("pfe", [128, 128], BF16)
    pm1 = c.psum("pm1", [128, 512], F32)
    pax = c.psum("pax", [128, 208], F32)
    pa = pax[0:80, 0:128]
    psT = pax[:, 128:208]

    @block.vector
    def _(v):
        c.inc(sV, v.memset(BD1[:], 0.0))
        c.inc(sV, v.memset(he_row[:], 0.0))
        c.inc(sV, v.memset(bd2[:], 0.0))
        for i in range(4):
            c.inc(sV, v.memset(W5[i][:], 0.0))

    @block.gpsimd
    def _(g):
        c.w(g, sV)
        for h in range(H):
            mt, mo = divmod(h * 32, 128)
            c.inc(sL, g.dma_start(out=BD1[h * 8:h * 8 + 8, mt * 128 + mo:mt * 128 + mo + 32],
                                  in_=I["mlp_w1"][:, :]), 16)
        for j in range(4):
            c.inc(sL, g.dma_start(out=bd2[32 * j:32 * j + 32, 32 * j:32 * j + 32],
                                  in_=I["mlp_w2"][:, :]), 16)
        for j in range(4):
            c.inc(sL, g.dma_start(out=b1col[32 * j:32 * (j + 1), 0:1],
                                  in_=I["mlp_b1"].rearrange("(f o) -> f o", o=1)), 16)
        c.inc(sL, g.dma_start(out=he_row[0:1, :].rearrange("o (h e) -> o h e", e=8)[:, :, 4:8],
                              in_=I["head_embed"][None, :, :], allow_slow_non_contiguous=True), 16)
        for i, n in enumerate(["bb_w", "bd_w", "lam_w", "ab_w", "ad_w"]):
            c.inc(sL, g.dma_start(out=uvec[0:1, i * 32:(i + 1) * 32], in_=I[n][None, :]), 16)
        for i, n in enumerate(["bb_b", "bd_b", "lam_b", "ab_b", "ad_b"]):
            c.inc(sL, g.dma_start(out=bscal[0:1, i:i + 1], in_=I[n][None, :]), 16)

    @block.vector
    def _(v):
        c.w(v, sL)
        def ub(i):
            return uvec[0:1, i * 32:(i + 1) * 32]
        def ur(i):
            return urow[0:1, i * 32:(i + 1) * 32]
        c.inc(sV, v.tensor_tensor(ur(0), ub(0), ub(1), ALU.add))
        c.inc(sV, v.tensor_tensor(ur(1), ub(0), ub(1), ALU.subtract))
        c.inc(sV, v.tensor_copy(ur(2), ub(2)))
        c.inc(sV, v.tensor_tensor(ur(3), ub(3), ub(4), ALU.add))
        c.inc(sV, v.tensor_tensor(ur(4), ub(3), ub(4), ALU.subtract))
        c.inc(sV, v.tensor_tensor(bcomb[0:1, 0:1], bscal[0:1, 0:1], bscal[0:1, 1:2], ALU.add))
        c.inc(sV, v.tensor_tensor(bcomb[0:1, 1:2], bscal[0:1, 0:1], bscal[0:1, 1:2], ALU.subtract))
        c.inc(sV, v.tensor_copy(bcomb[0:1, 2:3], bscal[0:1, 2:3]))
        c.inc(sV, v.tensor_tensor(bcomb[0:1, 3:4], bscal[0:1, 3:4], bscal[0:1, 4:5], ALU.add))
        c.inc(sV, v.tensor_tensor(bcomb[0:1, 4:5], bscal[0:1, 3:4], bscal[0:1, 4:5], ALU.subtract))

    @block.gpsimd
    def _(g):
        c.w(g, sV)
        c.inc(sP, g.dma_start(out=DR["usmall"][:, :], in_=urow[:, :]), 16)
        c.inc(sP, g.dma_start(out=DR["bsmall"][:, :], in_=bcomb[:, :]), 16)
        g.wait_ge(c.S(sP), c.cnt[sP])
        for cc in range(4):
            for hl in range(4):
                h = cc * 4 + hl
                for s5 in range(5):
                    c.inc(sP, g.dma_start(
                        out=W5[cc][hl * 32:(hl + 1) * 32, s5 * 16 + h:s5 * 16 + h + 1],
                        in_=DR["usmall"][0:1, s5 * 32:(s5 + 1) * 32].rearrange("o g -> g o"),
                        allow_slow_non_contiguous=True), 16)
        for s5 in range(5):
            c.inc(sP, g.dma_start(out=bias80[s5 * 16:(s5 + 1) * 16, 0:1],
                                  in_=DR["bsmall"][0:1, s5:s5 + 1].rearrange("o g -> g o").to_broadcast([16, 1]),
                                  allow_slow_non_contiguous=True), 16)
        for j in range(4):
            c.inc(sP, g.dma_start(out=b2col[32 * j:32 * (j + 1), 0:1],
                                  in_=I["mlp_b2"].rearrange("(f o) -> f o", o=1)), 16)

    @block.tensor
    def _(t):
        c.w(t, sL)
        c.inc(sT, t.matmul(pm1[:, 0:128], c.sb["onesb_row"][0:1, 0:128], he_row[0:1, :],
                           start=True, stop=True))
    @block.vector
    def _(v):
        c.w(v, sT)
        c.inc(sV, v.tensor_copy(he_b[:], pm1[:, 0:128]))

    for tt in range(NCH):
        @block.vector
        def _(v, tt=tt):
            c.w(v, sT_stats)
            c.w(v, sP)
            c.w(v, sS)
            fv = feats[:].rearrange("p (h f) -> p h f", f=8)
            sv = stats_sb[:].rearrange("p (h s) -> p h s", s=3)
            c.inc(sV, v.tensor_copy(stats_sb[:], pstat[:, tt * 48:(tt + 1) * 48]))
            c.w(v, sV)
            c.inc(sV, v.tensor_tensor(errsq[:], sv[:, :, 0], sv[:, :, 2], ALU.add))
            c.w(v, sV)
            c.inc(sV, v.scalar_tensor_tensor(errsq[:], sv[:, :, 1], -2.0, errsq[:], ALU.mult, ALU.add))
            c.w(v, sV)
            c.inc(sV, v.tensor_scalar_max(errsq[:], errsq[:], 0.0))
        @block.scalar
        def _(s, tt=tt):
            c.w(s, sV)
            sv = stats_sb[:].rearrange("p (h s) -> p h s", s=3)
            c.inc(sS, s.activation(vn[:], sv[:, :, 0], AF.Sqrt))
            c.inc(sS, s.activation(hn[:], sv[:, :, 2], AF.Sqrt))
            c.inc(sS, s.activation(en[:], errsq[:], AF.Sqrt))
        @block.vector
        def _(v, tt=tt):
            c.w(v, sS)
            fv = feats[:].rearrange("p (h f) -> p h f", f=8)
            sv = stats_sb[:].rearrange("p (h s) -> p h s", s=3)
            c.inc(sV, v.tensor_copy(feats[:], he_b[:]))
            c.inc(sV, v.tensor_scalar_add(r1[:], vn[:], 1e-6))
            c.w(v, sV)
            c.inc(sV, v.reciprocal(r1[:], r1[:]))
            c.w(v, sV)
            c.inc(sV, v.tensor_tensor(fv[:, :, 1], en[:], r1[:], ALU.mult))
            c.inc(sV, v.tensor_tensor(d2[:], vn[:], hn[:], ALU.mult))
            c.w(v, sV)
            c.inc(sV, v.tensor_scalar_add(d2[:], d2[:], 1e-6))
            c.w(v, sV)
            c.inc(sV, v.reciprocal(d2[:], d2[:]))
            c.w(v, sV)
            c.inc(sV, v.tensor_tensor(fv[:, :, 2], sv[:, :, 1], d2[:], ALU.mult))
            c.inc(sV, v.tensor_copy(fv[:, :, 0], entcol[:, tt:tt + 1].to_broadcast([128, 16])))
        @block.scalar
        def _(s, tt=tt):
            c.w(s, sV)
            fv = feats[:].rearrange("p (h f) -> p h f", f=8)
            c.inc(sS, s.activation(fv[:, :, 3], en[:], AF.Ln, bias=c.cc(1.0)))
        @block.tensor
        def _(t, tt=tt):
            c.w(t, sS)
            c.inc(sT, t.transpose(pfe[:], feats[:], ident[:]))
        @block.scalar
        def _(s, tt=tt):
            c.w(s, sT)
            c.inc(sS, s.copy(featsT[:], pfe[:]))
        @block.tensor
        def _(t, tt=tt):
            c.w(t, sS)
            for mt in range(4):
                c.inc(sT, t.matmul(pm1[:, mt * 128:(mt + 1) * 128], BD1[:, mt * 128:(mt + 1) * 128],
                                   featsT[:], start=True, stop=True))
        @block.scalar
        def _(s, tt=tt):
            c.w(s, sT)
            for mt in range(4):
                c.inc(sS, s.activation(hfT[:, mt * 128:(mt + 1) * 128], pm1[:, mt * 128:(mt + 1) * 128],
                                       AF.Silu, bias=b1col[:]))
        @block.tensor
        def _(t, tt=tt):
            c.w(t, sS)
            for mt in range(4):
                c.inc(sT, t.matmul(pm1[:, mt * 128:(mt + 1) * 128], bd2[:],
                                   hfT[:, mt * 128:(mt + 1) * 128], start=True, stop=True))
        @block.scalar
        def _(s, tt=tt):
            c.w(s, sT)
            for mt in range(4):
                c.inc(sS, s.activation(hf2T[:, mt * 128:(mt + 1) * 128], pm1[:, mt * 128:(mt + 1) * 128],
                                       AF.Silu, bias=b2col[:]))
        @block.tensor
        def _(t, tt=tt):
            c.w(t, sS)
            for cc in range(4):
                ins = t.matmul(pa, W5[cc][:], hf2T[:, cc * 128:(cc + 1) * 128],
                               start=(cc == 0), stop=(cc == 3))
            c.inc(sT, ins)
        @block.scalar
        def _(s, tt=tt):
            c.w(s, sT)
            c.w(s, sW)
            c.inc(sS, s.activation(sig[:], pa, AF.Sigmoid, bias=bias80[:]))
        @block.tensor
        def _(t, tt=tt):
            c.w(t, sS)
            c.inc(sT, t.transpose(psT, sig[:], c.sb["identf32"][0:80, 0:80]))
        @block.scalar
        def _(s, tt=tt):
            c.w(s, sT)
            c.inc(sS, s.copy(sigT[:], psT))
        @block.sync
        def _(s, tt=tt):
            c.w(s, sS)
            c.inc(sW, s.dma_start(out=DR["scal_row"][:, tt * 128:(tt + 1) * 128], in_=sig[:]), 16)
            c.inc(sW, s.dma_start(out=DR["scal_tok"][tt * 128:(tt + 1) * 128, :], in_=sigT[:]), 16)
            if t_scal is not None:
                c.inc(sW, s.dma_start(out=t_scal[:, tt * 128:(tt + 1) * 128], in_=sig[:]), 16)



def emit_pair_gathers_a(c, block, DR):
    groups = [[0, 1], [2, 3], [4, 5], [6, 7]]
    @block.gpsimd
    def _(g):
        c.w(g, "apW")
        c.w(g, "fmW")
        for name, gname in [("qf", "qg"), ("kf", "kg"), ("vf", "vg"),
                            ("scal_row", "scal_rowg"), ("scal_tok", "scal_tokg")]:
            c.inc("cc1", g.collective_compute(
                "AllGather", ALU.bypass, replica_groups=groups,
                ins=[DR[name].opt()], outs=[DR[gname].opt()]))
        g.wait_ge(c.S("cc1"), c.cnt["cc1"])


def emit_scan(c, block, I, DR):
    nc = c.nc
    mk = c.mark()
    identb = c.sb["identbf"]
    identf = c.sb["identf32"]
    onesf_row = c.sb["onesf_row"]
    ones_col = c.sb["onesb_col"]
    NCK = T // 128
    NG = 4
    GTOK = T // NG
    BIG = [64, 32, 16]
    SMALL = [8, 4, 2, 1]

    k_u = c.sbuf("sc_ku", [128, T], BF16)
    q_s = c.sbuf("sc_qs", [128, T], BF16)
    kbn = c.sbuf("sc_kbn", [128, T], BF16)
    prod = c.sbuf("sc_prod", [128, T], BF16)
    GA = c.sbuf("sc_GA", [128, T + 1], F32)
    bb = c.sbuf("sc_bb", [128, T], BF16)
    amp_all = c.sbuf("sc_ampall", [16, T], F32)
    amp_pp = c.sbuf("sc_amppp", [16, T], F32)
    arow = c.sbuf("sc_arow", [1, T], F32)
    brow = c.sbuf("sc_brow", [1, T], F32)
    krow = c.sbuf("sc_krow", [1, T], F32)
    lrow = krow
    rowbf = c.sbuf("sc_rowbf", [1, T], BF16)
    growrow = c.sbuf("sc_growrow", [1, 128], F32)
    grow_b = c.sbuf("sc_growb", [128, 128], F32)
    gcol = c.sbuf("sc_gcol", [128, 1], F32)
    qk_sb = c.sbuf("sc_qk", [128, NCK], F32)
    qkb_col = c.sbuf("sc_qkb", [128, 1], F32)
    bcol = c.sbuf("sc_bcol", [128, 1], F32)
    lcol = c.sbuf("sc_lcol", [128, 1], F32)
    gamcol = c.sbuf("sc_gamcol", [128, 1], F32)
    klc = {l: c.sbuf(f"sc_klc{l}", [128, GTOK // 2], BF16) for l in BIG}
    krc = {l: c.sbuf(f"sc_krc{l}", [128, GTOK // 2], BF16) for l in BIG}
    qlc = {l: c.sbuf(f"sc_qlc{l}", [128, GTOK // 2], BF16) for l in BIG}
    klw = {l: c.sbuf(f"sc_klw{l}", [128, GTOK], BF16) for l in SMALL}
    krw = {l: c.sbuf(f"sc_krw{l}", [128, GTOK], BF16) for l in SMALL}
    qlw = {l: c.sbuf(f"sc_qlw{l}", [128, GTOK], BF16) for l in SMALL}
    escr = c.sbuf("sc_escr", [128, GTOK // 2], F32)
    e0 = c.sbuf("sc_e0", [128, 128], BF16)
    wn = c.sbuf("sc_wn", [128, 128], BF16)
    qw = c.sbuf("sc_qw", [128, 128], BF16)
    vT = c.sbuf("sc_vT", [128, 128], BF16)
    bv = c.sbuf("sc_bv", [128, 128], BF16)
    kef = c.sbuf("sc_kef", [128, 128], BF16)
    keT = c.sbuf("sc_keT", [128, 128], BF16)
    Bsb = c.sbuf("sc_Bsb", [128, 128], BF16)
    BTsb = c.sbuf("sc_BTsb", [128, 128], BF16)
    B2sb = c.sbuf("sc_B2sb", [128, 128], BF16)
    B2Tsb = c.sbuf("sc_B2Tsb", [128, 128], BF16)
    Psb = c.sbuf("sc_Psb", [128, 128], BF16)
    msk = c.sbuf("sc_msk", [128, 4 * 128], F32)
    u_bf = c.sbuf("sc_ubf", [128, 128], BF16)
    S_f = c.sbuf("sc_Sf", [128, 128], F32)
    S_b = c.sbuf("sc_Sb", [128, 128], BF16)
    osb = c.sbuf("sc_osb", [128, 128], F32)
    of_t = c.sbuf("sc_oft", [128, 128], F32)
    ocmb = c.sbuf("sc_ocmb", [128, 128], F32)
    tmpc = c.sbuf("sc_tmpc", [128, 128], F32)
    pBig = c.psum("sc_pBig", [128, 512], F32)
    pSmB = c.psum("sc_pSmB", [128, 512], F32)
    pSmP = c.psum("sc_pSmP", [128, 512], F32)
    pMis = c.psum("sc_pMis", [128, 512], F32)
    pTrb = c.psum("sc_pTrb", [128, 256], BF16)
    pRow = c.psum("sc_pRow", [1, 512], F32)
    pQk = c.psum("sc_pQk", [128, 64], F32)
    pCol = c.psum("sc_pCol", [128, 8], F32)

    vB = pBig[:, 0:128]; vP = pBig[:, 128:256]; pu = pBig[:, 256:384]; po = pBig[:, 384:512]

    sY = "scY"; sV = "scV"; sS = "scS"; sT = "scT"; sP = "scP"; sW = "scW"; sX = "scX"

    bigmask = c.sbuf("bigmask_sb", [128, 128], F32)
    mask4 = c.sbuf("mask4_sb", [128, 512], F32)
    @block.sync
    def _(s):
        c.inc(sY, s.dma_start(out=bigmask[:], in_=I["bigmask"][:, :]), 16)
        c.inc(sY, s.dma_start(out=mask4[:], in_=I["mask4"][:, :]), 16)

    gchT = c.sbuf("sc_gchT", [16, 128], F32)
    spT = c.sbuf("sc_spT", [16, 128], F32)
    absT = c.sbuf("sc_absT", [16, 128], F32)
    aexp = c.sbuf("sc_aexp", [16, 1], F32)
    gch = c.sbuf("sc_gch", [128, 16], F32)
    @block.sync
    def _(s):
        for dep in ["fmW", "apW", "ftW", "ftL", "fmP", "fmL"]:
            if dep in c.cnt:
                c.w(s, dep)
        c.w(s, sY)
        c.inc(sY, s.dma_start(out=spT[:], in_=I["dt_bias"].rearrange("(h k) -> h k", k=128)), 16)
        c.inc(sY, s.dma_start(out=aexp[:], in_=I["A_log"].rearrange("(h o) -> h o", o=1)), 16)
    @block.scalar
    def _(s):
        c.w(s, sY)
        c.inc(sS, s.activation(absT[:], spT[:], AF.Abs))
        c.w(s, sS)
        c.inc(sS, s.activation(absT[:], absT[:], AF.Exp, scale=c.cc(-1.0)[0:16]))
        c.w(s, sS)
        c.inc(sS, s.activation(absT[:], absT[:], AF.Ln, bias=c.cc(1.0)[0:16]))
        c.inc(sS, s.activation(aexp[:], aexp[:], AF.Exp))
        c.w(s, sS)
        c.inc(sS, s.mul(aexp[:], aexp[:], c.cc(-1.0)[0:16]))
    @block.vector
    def _(v):
        c.w(v, sS)
        c.inc(sV, v.tensor_scalar_max(spT[:], spT[:], 0.0))
        c.w(v, sV)
        c.inc(sV, v.tensor_tensor(spT[:], spT[:], absT[:], ALU.add))
        c.w(v, sV)
        c.inc(sV, v.tensor_scalar_mul(gchT[:], spT[:], aexp[:]))
    @block.tensor
    def _(t):
        c.w(t, sV)
        c.inc(sT, t.transpose(pMis[0:128, 0:16], gchT[:], identf[0:16, 0:16]))
    @block.vector
    def _(v):
        c.w(v, sT)
        c.inc(sV, v.tensor_copy(gch[:], pMis[0:128, 0:16]))
    @block.gpsimd
    def _(g):
        c.w(g, sV)
        c.inc(sP, g.dma_start(out=DR["gchT_d"][:, :], in_=gchT[:]), 16)

    idx_kqv0 = c.sbuf("idx_kqv0", [128, 1], mybir.dt.int32)
    idx_kqv1 = c.sbuf("idx_kqv1", [128, 1], mybir.dt.int32)
    idx_amp4 = [c.sbuf(f"idx_amp{i}", [8, 1], mybir.dt.int32) for i in range(4)]
    idx_one = c.sbuf("idx_one", [1, 1], mybir.dt.int32)
    idx_one2 = c.sbuf("idx_one2", [1, 1], mybir.dt.int32)
    idx_sr = c.sbuf("idx_sr", [8, 1], mybir.dt.int32)
    idx_gc = c.sbuf("idx_gc", [8, 1], mybir.dt.int32)
    sc_rows = c.sbuf("sc_rows", [80, NTOK], F32)
    gch_rows = c.sbuf("gch_rows", [8, 128], F32)
    @block.sync
    def _(s):
        c.w(s, "cc1")
        c.w(s, sY)
        for i in range(4):
            c.inc(sY, s.dma_start(out=idx_amp4[i][:],
                                  in_=I["amp_idx"][0:1, i * 8:(i + 1) * 8].rearrange("o k -> k o"),
                                  allow_slow_non_contiguous=True), 16)
    @block.gpsimd
    def _(g):
        c.w(g, sY)
        c.w(g, "scP")
        for sp in range(2):
            for half in range(2):
                blk = sp * 2 + half
                c.inc("scSt", g.indirect_dma_start(
                    amp_all[sp * 8:sp * 8 + 8, half * NTOK:(half + 1) * NTOK], None,
                    DR["scal_rowg"][:, :],
                    IndirectOffsetOnAxis(idx_amp4[blk][:, 0:1], 0)), 16)
        for s5 in range(5):
            for half in range(2):
                blk = s5 * 2 + half
                g.wait_ge(c.S("scSt"), c.cnt["scSt"])
                c.inc("scSt", g.dma_start(out=idx_sr[:],
                                          in_=I["srow_idx"][0:1, blk * 8:(blk + 1) * 8].rearrange("o k -> k o"),
                                          allow_slow_non_contiguous=True), 16)
                g.wait_ge(c.S("scSt"), c.cnt["scSt"])
                c.inc("scSt", g.indirect_dma_start(
                    sc_rows[blk * 8:(blk + 1) * 8, :], None, DR["scal_rowg"][:, :],
                    IndirectOffsetOnAxis(idx_sr[:, 0:1], 0)), 16)
        c.inc("scSt", g.dma_start(out=idx_gc[:],
                                  in_=I["gch_idx"][0:1, :].rearrange("o k -> k o"),
                                  allow_slow_non_contiguous=True), 16)
        g.wait_ge(c.S("scSt"), c.cnt["scSt"])
        c.inc("scSt", g.indirect_dma_start(
            gch_rows[:, :], None, DR["gchT_d"][:, :],
            IndirectOffsetOnAxis(idx_gc[:, 0:1], 0)), 16)
        g.wait_ge(c.S("scSt"), c.cnt["scSt"])
        c.inc("scSt", g.dma_start(out=DR["gmy_d"][:, :], in_=gch_rows[:, :]), 16)
        g.wait_ge(c.S("scSt"), c.cnt["scSt"])
    @block.vector
    def _(v):
        c.w(v, "scSt")
        cur, nxt = amp_all, amp_pp
        sh = 1
        while sh < T:
            c.inc(sV, v.tensor_copy(nxt[:, 0:sh], cur[:, 0:sh]))
            c.inc(sV, v.tensor_tensor(nxt[:, sh:T], cur[:, sh:T], cur[:, 0:T - sh], ALU.add))
            c.w(v, sV)
            cur, nxt = nxt, cur
            sh *= 2
        if cur is not amp_all:
            c.inc(sV, v.tensor_copy(amp_all[:], cur[:]))
            c.w(v, sV)

    emit_scan_heads(c, block, I, DR, locals())
    c.release(mk)


def emit_scan_heads(c, block, I, DR, E):
    nc = c.nc
    g = lambda n: E[n]
    identb, identf = g("identb"), g("identf")
    onesf_row, ones_col = g("onesf_row"), g("ones_col")
    NCK, NG, GTOK, BIG, SMALL = g("NCK"), g("NG"), g("GTOK"), g("BIG"), g("SMALL")
    k_u, q_s, kbn, prod, GA, bb = g("k_u"), g("q_s"), g("kbn"), g("prod"), g("GA"), g("bb")
    amp_all, arow, brow, krow, lrow, rowbf = g("amp_all"), g("arow"), g("brow"), g("krow"), g("lrow"), g("rowbf")
    growrow, grow_b, gcol = g("growrow"), g("grow_b"), g("gcol")
    qk_sb, qkb_col, bcol, lcol, gamcol = g("qk_sb"), g("qkb_col"), g("bcol"), g("lcol"), g("gamcol")
    klc, krc, qlc, klw, krw, qlw, escr = g("klc"), g("krc"), g("qlc"), g("klw"), g("krw"), g("qlw"), g("escr")
    e0, wn, qw, vT, bv, kef, keT = g("e0"), g("wn"), g("qw"), g("vT"), g("bv"), g("kef"), g("keT")
    Bsb, BTsb, B2sb, B2Tsb, Psb, msk = g("Bsb"), g("BTsb"), g("B2sb"), g("B2Tsb"), g("Psb"), g("msk")
    u_bf, S_f, S_b, osb, of_t, ocmb, tmpc = g("u_bf"), g("S_f"), g("S_b"), g("osb"), g("of_t"), g("ocmb"), g("tmpc")
    pBig, pSmB, pSmP, pMis, pTrb, pRow, pQk, pCol = g("pBig"), g("pSmB"), g("pSmP"), g("pMis"), g("pTrb"), g("pRow"), g("pQk"), g("pCol")
    vB, vP, pu, po = g("vB"), g("vP"), g("pu"), g("po")
    gch = c.sb["sc_gch"]
    idx_kqv0, idx_kqv1 = E["idx_kqv0"], E["idx_kqv1"]
    idx_one, idx_one2 = E["idx_one"], E["idx_one2"]
    sc_rows, gch_rows = E["sc_rows"], E["gch_rows"]
    sY, sV, sS, sT, sP, sW, sX = g("sY"), g("sV"), g("sS"), g("sT"), g("sP"), g("sW"), g("sX")

    def dynreg(eng, key):
        if key not in c.sb:
            pid = eng.partition_id()
            hb = eng.compute_val((pid % 2) * 8)
            rr = eng.alloc_register(key)
            rc = eng.alloc_register(key + "_c")
            wraps = {"row": RuntimeValue(rr, 0, 4096 - 128), "r159": RuntimeValue(rr, 0, 159),
                     "r15": RuntimeValue(rr, 0, 15), "rcreg": rc,
                     "col": RuntimeValue(rc, 0, 2048 - 128)}
            c.sb[key] = (hb, rr, wraps)
        return c.sb[key]
    E["dynreg"] = dynreg

    for j in range(8):
        @block.sync
        def _(s, j=j):
            c.w(s, "scSt")
            c.w(s, sY)
            c.w(s, sV)
            c.w(s, sT)
            c.inc(sY, s.dma_start(out=idx_kqv0[:],
                                  in_=I["kqv_idx"][j:j + 1, 0:128].rearrange("o k -> k o"),
                                  allow_slow_non_contiguous=True), 16)
            c.inc(sY, s.dma_start(out=idx_kqv1[:],
                                  in_=I["kqv_idx"][j:j + 1, 128:256].rearrange("o k -> k o"),
                                  allow_slow_non_contiguous=True), 16)
        @block.gpsimd
        def _(g, j=j):
            c.w(g, sY)
            c.w(g, sV)
            c.w(g, sT)
            c.w(g, "scP")
            for half in range(2):
                iv = (idx_kqv0 if half == 0 else idx_kqv1)[:, 0:1]
                c.inc("scP", g.indirect_dma_start(
                    k_u[:, half * NTOK:(half + 1) * NTOK], None, DR["kg"][:, :],
                    IndirectOffsetOnAxis(iv, 0)), 16)
                c.inc("scP", g.indirect_dma_start(
                    q_s[:, half * NTOK:(half + 1) * NTOK], None, DR["qg"][:, :],
                    IndirectOffsetOnAxis(iv, 0)), 16)
                c.inc("scP", g.indirect_dma_start(
                    bb[:, half * NTOK:(half + 1) * NTOK], None, DR["vg"][:, :],
                    IndirectOffsetOnAxis(iv, 0)), 16)
            g.wait_ge(c.S("scP"), c.cnt["scP"])
            c.inc("scP", g.dma_start(out=DR["vhd"][:, :], in_=bb[:]), 16)
            c.inc("scP", g.dma_start(out=growrow[0:1, :], in_=gch_rows[j:j + 1, :]), 16)
            c.inc("scP", g.dma_start(out=gcol[:, 0:1],
                                     in_=DR["gmy_d"][j:j + 1, :].rearrange("o k -> k o"),
                                     allow_slow_non_contiguous=True), 16)
            g.wait_ge(c.S("scP"), c.cnt["scP"])
        @block.scalar
        def _(s, j=j):
            c.w(s, sY)
            c.w(s, sV)
            c.w(s, "scP")
            c.inc(sS, s.activation(prod[:], k_u[:], AF.Square))
        for nt in range(8):
            @block.tensor
            def _(t, nt=nt):
                c.w(t, sS)
                c.w(t, sV)
                c.inc(sT, t.matmul(pRow[0:1, 0:512], ones_col[:],
                                   prod[:, nt * 512:(nt + 1) * 512], start=True, stop=True))
            @block.vector
            def _(v, nt=nt):
                c.w(v, sT)
                c.inc(sV, v.tensor_copy(krow[0:1, nt * 512:(nt + 1) * 512], pRow[0:1, 0:512]))
        @block.scalar
        def _(s, j=j):
            c.w(s, sV)
            c.inc(sS, s.activation(krow[0:1, :], krow[0:1, :], AF.Sqrt))
        @block.vector
        def _(v, j=j):
            c.w(v, sS)
            c.inc(sV, v.tensor_scalar_add(krow[0:1, :], krow[0:1, :], 1e-6))
            c.w(v, sV)
            c.inc(sV, v.reciprocal(krow[0:1, :], krow[0:1, :]))
            c.w(v, sV)
            c.inc(sV, v.tensor_copy(rowbf[0:1, :], krow[0:1, :]))
        emit_bcast_row(c, block, rowbf, bb, E, bf=True, deps=("scP",))
        @block.vector
        def _(v, j=j):
            c.w(v, sS)
            c.w(v, sT)
            c.inc(sV, v.tensor_tensor(k_u[:], k_u[:], bb[:], ALU.mult))
            c.w(v, sV)
            c.inc(sV, v.tensor_scalar_mul(q_s[:], q_s[:], DK ** (-0.5)))
            c.w(v, sV)
            c.inc(sV, v.tensor_tensor(prod[:], q_s[:], k_u[:], ALU.mult))
        @block.tensor
        def _(t, j=j):
            c.w(t, sV)
            for ck in range(NCK):
                c.inc(sT, t.matmul(pQk[:, ck * 2:ck * 2 + 1], prod[:, ck * 128:(ck + 1) * 128],
                                   ones_col[:], start=True, stop=True))
        @block.vector
        def _(v, j=j):
            c.w(v, sT)
            c.inc(sV, v.tensor_copy(qk_sb[:], pQk[:, 0:2 * NCK].rearrange("p (ck o) -> p ck o", o=2)[:, :, 0]))
        @block.gpsimd
        def _(g, j=j):
            c.w(g, sV)
            c.w(g, "scP")
            c.w(g, sT)
            for half in range(2):
                blk = 2 * 2 + half
                c.inc("scP", g.dma_start(out=krow[0:1, half * NTOK:(half + 1) * NTOK],
                                         in_=sc_rows[blk * 8 + j:blk * 8 + j + 1, :]), 16)
            g.wait_ge(c.S("scP"), c.cnt["scP"])


        for sp in range(2):
            @block.gpsimd
            def _(gg, j=j, sp=sp):
                c.w(gg, sV)
                c.w(gg, sT)
                c.inc(sP, gg.dma_start(out=arow[0:1, :], in_=amp_all[sp * 8 + j:sp * 8 + j + 1, :]), 16)
            @block.gpsimd
            def _(g, j=j, sp=sp):
                c.w(g, "scP")
                c.w(g, sT)
                for half in range(2):
                    blk = sp * 2 + half
                    c.inc("scP", g.dma_start(out=brow[0:1, half * NTOK:(half + 1) * NTOK],
                                             in_=sc_rows[blk * 8 + j:blk * 8 + j + 1, :]), 16)
                g.wait_ge(c.S("scP"), c.cnt["scP"])
            emit_bcast_row(c, block, arow, GA, E, bf=False, out_off=1, deps=(sP, sY))
            @block.vector
            def _(v, j=j, sp=sp):
                c.w(v, sP)
                c.w(v, sT)
                c.w(v, sS)
                c.inc(sV, v.memset(GA[:, 0:1], 0.0))
                c.w(v, sV)
                c.inc(sV, v.tensor_scalar_mul(GA[:], GA[:], gcol[:, 0:1]))
                c.w(v, sP)
                c.inc(sV, v.tensor_copy(rowbf[0:1, :], brow[0:1, :]))
            emit_bcast_row(c, block, rowbf, bb, E, bf=True, deps=(sP, sY))
            @block.vector
            def _(v, j=j, sp=sp):
                c.w(v, sT)
                c.w(v, sS)
                c.inc(sV, v.tensor_tensor(kbn[:], bb[:], k_u[:], ALU.mult))
                c.w(v, sV)
                c.inc(sV, v.tensor_scalar_mul(kbn[:], kbn[:], -1.0))

            for grp in range(NG):
                emit_group_tiles(c, block, E, grp)
                for ck in range(grp * (NCK // NG), (grp + 1) * (NCK // NG)):
                    emit_chunk(c, block, I, DR, E, j, sp, ck, grp)

    c.sb.pop("_dynY", None)


def emit_bcast_row(c, block, src_row, dst, E, bf, out_off=0, deps=()):
    sT, sV, sS = E["sT"], E["sV"], E["sS"]
    onesf_row, pMis = E["onesf_row"], E["pMis"]
    ones_brow = c.sb["onesb_row"]
    for nt in range(8):
        @block.tensor
        def _(t, nt=nt):
            c.w(t, sV)
            c.w(t, sS)
            for d in deps:
                c.w(t, d)
            lhs = ones_brow[0:1, 0:128] if bf else onesf_row[0:1, :]
            c.inc(sT, t.matmul(pMis[:, 0:512] if not bf else pMis[:, 0:512],
                               lhs, src_row[0:1, nt * 512:(nt + 1) * 512],
                               start=True, stop=True))
        @block.scalar
        def _(s, nt=nt):
            c.w(s, sT)
            c.inc(sS, s.copy(dst[:, out_off + nt * 512:out_off + (nt + 1) * 512], pMis[:, 0:512]))


def emit_group_tiles(c, block, E, grp):
    GA, k_u, q_s, kbn, escr = E["GA"], E["k_u"], E["q_s"], E["kbn"], E["escr"]
    klc, krc, qlc = E["klc"], E["krc"], E["qlc"]
    klw, krw, qlw = E["klw"], E["krw"], E["qlw"]
    GTOK, BIG, SMALL = E["GTOK"], E["BIG"], E["SMALL"]
    sV, sS = E["sV"], E["sS"]
    g0 = grp * GTOK

    def tview(tile, l, off=0, which=1):
        return tile[:, off + g0:off + g0 + GTOK].rearrange(
            "p (b two l) -> p b two l", two=2, l=l)[:, :, which, :]

    def refview(l):
        nb = GTOK // (2 * l)
        return E["GA"][:, g0:g0 + GTOK].rearrange(
            "p (b x) -> p b x", x=2 * l)[:, :, l:l + 1].to_broadcast([128, nb, l])

    for l in BIG + SMALL:
        nb = GTOK // (2 * l)
        big = l in BIG
        esl = escr[:].rearrange("p (b l) -> p b l", l=l)[:, 0:nb, :]
        if big:
            kl_t = klc[l][:].rearrange("p (b l) -> p b l", l=l)[:, 0:nb, :]
            kr_t = krc[l][:].rearrange("p (b l) -> p b l", l=l)[:, 0:nb, :]
            ql_t = qlc[l][:].rearrange("p (b l) -> p b l", l=l)[:, 0:nb, :]
        else:
            @block.vector
            def _(v, l=l):
                c.w(v, E["sT"])
                c.inc(sV, v.memset(klw[l][:], 0.0))
                c.inc(sV, v.memset(krw[l][:], 0.0))
                c.inc(sV, v.memset(qlw[l][:], 0.0))
            kl_t = klw[l][:, 0:GTOK].rearrange("p (b two l) -> p b two l", two=2, l=l)[:, :, 1, :]
            kr_t = krw[l][:, 0:GTOK].rearrange("p (b two l) -> p b two l", two=2, l=l)[:, :, 0, :]
            ql_t = qlw[l][:, 0:GTOK].rearrange("p (b two l) -> p b two l", two=2, l=l)[:, :, 1, :]
        gaT = tview(GA, l, off=1, which=1)
        gaS = tview(GA, l, off=1, which=0)
        rv = refview(l)
        kbnT = tview(kbn, l, which=1)
        qsT = tview(q_s, l, which=1)
        kuS = tview(k_u, l, which=0)
        @block.vector
        def _(v, l=l):
            c.w(v, sS)
            c.w(v, E["sT"])
            c.inc(sV, v.tensor_tensor(esl, gaT, rv, ALU.subtract))
        @block.scalar
        def _(s, l=l):
            c.w(s, sV)
            c.inc(sS, s.activation(ql_t, esl, AF.Exp))
        @block.vector
        def _(v, l=l):
            c.w(v, sS)
            c.inc(sV, v.tensor_tensor(kl_t, ql_t, kbnT, ALU.mult))
            c.w(v, sV)
            c.inc(sV, v.tensor_tensor(ql_t, ql_t, qsT, ALU.mult))
            c.w(v, sV)
            c.inc(sV, v.tensor_tensor(esl, rv, gaS, ALU.subtract))
        @block.scalar
        def _(s, l=l):
            c.w(s, sV)
            c.inc(sS, s.activation(kr_t, esl, AF.Exp))
        @block.vector
        def _(v, l=l):
            c.w(v, sS)
            c.inc(sV, v.tensor_tensor(kr_t, kr_t, kuS, ALU.mult))


def emit_chunk(c, block, I, DR, E, j, sp, ck, grp):
    nc = c.nc
    TAPS = c.taps if (j, sp, ck) == DBG_AT else {}
    identb, identf = E["identb"], E["identf"]
    GTOK, BIG, SMALL, NCK = E["GTOK"], E["BIG"], E["SMALL"], E["NCK"]
    GA, k_u, q_s, kbn = E["GA"], E["k_u"], E["q_s"], E["kbn"]
    klc, krc, qlc, klw, krw, qlw = E["klc"], E["krc"], E["qlc"], E["klw"], E["krw"], E["qlw"]
    e0, wn, qw, vT, bv, kef, keT = E["e0"], E["wn"], E["qw"], E["vT"], E["bv"], E["kef"], E["keT"]
    Bsb, BTsb, B2sb, B2Tsb, Psb, msk = E["Bsb"], E["BTsb"], E["B2sb"], E["B2Tsb"], E["Psb"], E["msk"]
    u_bf, S_f, S_b, osb, of_t, ocmb, tmpc = E["u_bf"], E["S_f"], E["S_b"], E["osb"], E["of_t"], E["ocmb"], E["tmpc"]
    pBig, pSmB, pSmP, pMis, pTrb, pCol = E["pBig"], E["pSmB"], E["pSmP"], E["pMis"], E["pTrb"], E["pCol"]
    vB, vP, pu, po = E["vB"], E["vP"], E["pu"], E["po"]
    qk_sb, qkb_col, bcol, lcol, gamcol = E["qk_sb"], E["qkb_col"], E["bcol"], E["lcol"], E["gamcol"]
    brow, lrow = E["brow"], E["krow"]
    sY, sV, sS, sT, sP, sW = E["sY"], E["sV"], E["sS"], E["sT"], E["sP"], E["sW"]
    bigmask = c.sb["bigmask_sb"]
    mask4 = c.sb["mask4_sb"]
    c0 = ck * 128
    gl = c0 - grp * GTOK
    half = ck // (NCK // 2)
    tl = c0 - half * NTOK

    if ck == 0:
        @block.vector
        def _(v):
            c.w(v, sS)
            c.inc(sV, v.memset(S_f[:], 0.0))
            c.inc(sV, v.memset(u_bf[:], 0.0))
        @block.scalar
        def _(s):
            c.w(s, sV)
            c.inc(sS, s.copy(S_b[:], S_f[:]))

    @block.sync
    def _(s):
        c.w(s, sV)
        c.w(s, sY)
        c.w(s, "scP")
        if "dbgW" in c.cnt:
            c.w(s, "dbgW")
        c.inc(sY, s.dma_start(out=vT[:], in_=DR["vhd"][:, c0:c0 + 128],
                              transpose=True), 16)
    if sp == 1:
        @block.gpsimd
        def _(g):
            c.w(g, sV)
            c.inc(sP, g.dma_start(out=of_t[:], in_=DR["of_scr"][c0:c0 + 128, :]), 16)

    @block.vector
    def _(v):
        c.w(v, sS)
        c.w(v, sT)
        if "dbgW" in c.cnt:
            c.w(v, "dbgW")
        c.inc(sV, v.tensor_scalar(tmpc[:], GA[:, c0 + 1:c0 + 129], GA[:, c0:c0 + 1], None, ALU.subtract))
    @block.scalar
    def _(s):
        c.w(s, sV)
        c.inc(sS, s.activation(e0[:], tmpc[:], AF.Exp))
    @block.vector
    def _(v):
        c.w(v, sS)
        c.inc(sV, v.tensor_tensor(wn[:], e0[:], kbn[:, c0:c0 + 128], ALU.mult))
        c.inc(sV, v.tensor_tensor(qw[:], e0[:], q_s[:, c0:c0 + 128], ALU.mult))
        c.inc(sV, v.tensor_scalar(gamcol[:], GA[:, c0 + 128:c0 + 129], GA[:, c0:c0 + 1], None, ALU.subtract))
    @block.scalar
    def _(s):
        c.w(s, sV)
        c.inc(sS, s.activation(gamcol[:], gamcol[:], AF.Exp))
    @block.tensor
    def _(t):
        c.w(t, sV)
        c.inc(sT, t.transpose(pCol[:, 0:1], brow[0:1, c0:c0 + 128], identf[0:1, 0:1]))
        if sp == 1:
            c.inc(sT, t.transpose(pCol[:, 1:2], lrow[0:1, c0:c0 + 128], identf[0:1, 0:1]))
    @block.vector
    def _(v):
        c.w(v, sT)
        c.inc(sV, v.tensor_copy(bcol[:], pCol[:, 0:1]))
        if sp == 1:
            c.inc(sV, v.tensor_copy(lcol[:], pCol[:, 1:2]))
        c.w(v, sY)
        c.w(v, sV)
        c.inc(sV, v.tensor_scalar_mul(bv[:], vT[:], bcol[:, 0:1]))
        c.w(v, sV)
        c.inc(sV, v.tensor_copy(qkb_col[:], qk_sb[:, ck:ck + 1]))

    @block.vector
    def _(v):
        c.w(v, sT)
        c.w(v, sS)
        if "dbgW" in c.cnt:
            c.w(v, "dbgW")
        c.inc(sV, v.memset(vB, 0.0))
        c.inc(sV, v.memset(vP, 0.0))
    @block.tensor
    def _(t):
        c.w(t, sV)
        c.w(t, sS)
        first = True
        for l in BIG:
            for b in range(128 // (2 * l)):
                bg = gl // (2 * l) + b
                sb0 = 2 * l * b
                tp = (0, 96) if sb0 == 96 else None
                c.inc(sT, t.matmul(vB[sb0:sb0 + l, sb0 + l:sb0 + 2 * l],
                                   krc[l][:, bg * l:(bg + 1) * l],
                                   klc[l][:, bg * l:(bg + 1) * l],
                                   start=True, stop=True, tile_position=tp))
                c.inc(sT, t.matmul(vP[sb0:sb0 + l, sb0 + l:sb0 + 2 * l],
                                   krc[l][:, bg * l:(bg + 1) * l],
                                   qlc[l][:, bg * l:(bg + 1) * l],
                                   start=True, stop=True, tile_position=tp))
        for li, l in enumerate(SMALL):
            c.inc(sT, t.matmul(pSmB[:, li * 128:(li + 1) * 128],
                               krw[l][:, gl:gl + 128], klw[l][:, gl:gl + 128],
                               start=True, stop=True))
            c.inc(sT, t.matmul(pSmP[:, li * 128:(li + 1) * 128],
                               krw[l][:, gl:gl + 128], qlw[l][:, gl:gl + 128],
                               start=True, stop=True))

    for (big_ps, sm_ps, dst) in [(vB, pSmB, Bsb), (vP, pSmP, Psb)]:
        @block.vector
        def _(v, big_ps=big_ps, sm_ps=sm_ps, dst=dst):
            c.w(v, sT)
            c.w(v, sS)
            c.inc(sV, v.tensor_tensor(tmpc[:], big_ps, bigmask[:], ALU.mult))
            c.w(v, sV)
            c.inc(sV, v.tensor_tensor(msk[:], sm_ps[:, 0:512], mask4[:], ALU.mult))
            c.w(v, sV)
            for q in range(4):
                c.inc(sV, v.tensor_tensor(tmpc[:], tmpc[:], msk[:, q * 128:(q + 1) * 128], ALU.add))
                c.w(v, sV)
        @block.scalar
        def _(s, dst=dst):
            c.w(s, sV)
            c.inc(sS, s.copy(dst[:], tmpc[:]))

    @block.tensor
    def _(t):
        c.w(t, sS)
        c.w(t, sV)
        c.inc(sT, t.matmul(pu, wn[:], S_b[:], start=True, stop=False))
        c.inc(sT, t.matmul(pu, identb[:], bv[:], start=False, stop=True))
    @block.scalar
    def _(s):
        c.w(s, sT)
        c.inc(sS, s.copy(u_bf[:], pu))

    cur, curT, nxt, nxtT = Bsb, BTsb, B2sb, B2Tsb
    @block.tensor
    def _(t):
        c.w(t, sS)
        c.inc(sT, t.transpose(pTrb[:, 0:128], Bsb[:], identb[:]))
    @block.scalar
    def _(s):
        c.w(s, sT)
        c.inc(sS, s.copy(BTsb[:], pTrb[:, 0:128]))
    for k in range(7):
        @block.tensor
        def _(t, cur=cur):
            c.w(t, sS)
            c.inc(sT, t.matmul(pu, identb[:], u_bf[:], start=True, stop=False))
            c.inc(sT, t.matmul(pu, cur[:], u_bf[:], start=False, stop=True))
        @block.scalar
        def _(s):
            c.w(s, sT)
            c.inc(sS, s.copy(u_bf[:], pu))
        if k < 6:
            @block.tensor
            def _(t, cur=cur, curT=curT):
                c.w(t, sS)
                c.inc(sT, t.matmul(pMis[:, 0:128], curT[:], cur[:], start=True, stop=True))
            @block.scalar
            def _(s, nxt=nxt):
                c.w(s, sT)
                c.inc(sS, s.copy(nxt[:], pMis[:, 0:128]))
            @block.tensor
            def _(t, nxt=nxt):
                c.w(t, sS)
                c.inc(sT, t.transpose(pTrb[:, 128:256], nxt[:], identb[:]))
            @block.scalar
            def _(s, nxtT=nxtT):
                c.w(s, sT)
                c.inc(sS, s.copy(nxtT[:], pTrb[:, 128:256]))
            cur, curT, nxt, nxtT = nxt, nxtT, cur, curT

    @block.tensor
    def _(t):
        c.w(t, sS)
        c.inc(sT, t.matmul(po, qw[:], S_b[:], start=True, stop=False))
        c.inc(sT, t.matmul(po, Psb[:], u_bf[:], start=False, stop=True))
    @block.vector
    def _(v):
        c.w(v, sT)
        c.w(v, sP)
        c.w(v, "scW")
        c.inc(sV, v.scalar_tensor_tensor(osb[:], u_bf[:], qkb_col[:, 0:1], po, ALU.mult, ALU.add))

    @block.vector
    def _(v):
        c.w(v, sS)
        c.inc(sV, v.tensor_scalar(tmpc[:], GA[:, c0 + 1:c0 + 129], GA[:, c0 + 128:c0 + 129], None, ALU.subtract))
    @block.scalar
    def _(s):
        c.w(s, sV)
        c.inc(sS, s.activation(kef[:], tmpc[:], AF.Exp, scale=c.cc(-1.0)))
    @block.vector
    def _(v):
        c.w(v, sS)
        c.inc(sV, v.tensor_tensor(kef[:], kef[:], k_u[:, c0:c0 + 128], ALU.mult))
    @block.tensor
    def _(t):
        c.w(t, sV)
        c.inc(sT, t.transpose(pTrb[:, 0:128], kef[:], identb[:]))
    @block.scalar
    def _(s):
        c.w(s, sT)
        c.inc(sS, s.copy(keT[:], pTrb[:, 0:128]))
    @block.tensor
    def _(t):
        c.w(t, sS)
        c.inc(sT, t.matmul(pMis[:, 128:256], keT[:], u_bf[:], start=True, stop=True))
    @block.vector
    def _(v):
        c.w(v, sT)
        c.inc(sV, v.scalar_tensor_tensor(S_f[:], S_f[:], gamcol[:, 0:1], pMis[:, 128:256], ALU.mult, ALU.add))
    @block.scalar
    def _(s):
        c.w(s, sV)
        c.inc(sS, s.copy(S_b[:], S_f[:]))

    if sp == 0:
        @block.gpsimd
        def _(g):
            c.w(g, sV)
            c.inc(sP, g.dma_start(out=DR["of_scr"][c0:c0 + 128, :], in_=osb[:]), 16)
    else:
        @block.vector
        def _(v):
            c.w(v, sV)
            c.w(v, sP)
            c.inc(sV, v.tensor_scalar(tmpc[:, 0:1], lcol[:], -1.0, 1.0, ALU.mult, ALU.add))
            c.w(v, sV)
            c.inc(sV, v.tensor_scalar_mul(ocmb[:], osb[:], tmpc[:, 0:1]))
            c.w(v, sV)
            c.inc(sV, v.scalar_tensor_tensor(ocmb[:], of_t[:], lcol[:, 0:1], ocmb[:], ALU.mult, ALU.add))
        @block.gpsimd
        def _(g):
            c.w(g, sV)
            c.inc("scW", g.dma_start(out=DR["ostore"][j * T + c0:j * T + c0 + 128, :], in_=ocmb[:]), 16)
    if TAPS:
        @block.gpsimd
        def _(g):
            c.w(g, "scS"); c.w(g, "scV"); c.w(g, "scT"); c.w(g, "scW"); c.w(g, "scY"); c.w(g, "scP")
            for nm, tile in [("dGA", E["GA"][:, 0:260]), ("dB", E["Bsb"][:]), ("dP", E["Psb"][:]),
                             ("du", E["u_bf"][:]), ("dosb", E["osb"][:]), ("dS", E["S_f"][:]),
                             ("dku", E["k_u"][:, 0:256]), ("dqs", E["q_s"][:, 0:256]),
                             ("dbb", E["bb"][:, 0:256]), ("dvT", E["vT"][:]),
                             ("dkbn", E["kbn"][:, 0:256]), ("dkl8", E["klw"][8][:, 0:256]),
                             ("dkr8", E["krw"][8][:, 0:256]), ("dkl64", E["klc"][64][:, 0:256]),
                             ("dlcol", E["lcol"][:]), ("doft", E["of_t"][:]), ("docmb", E["ocmb"][:]),
                             ("dgcol", E["gcol"][:])]:
                if nm in TAPS:
                    c.inc("dbgW", g.dma_start(out=TAPS[nm][:, :], in_=tile), 16)
            g.wait_ge(c.S("dbgW"), c.cnt["dbgW"])



def emit_gather_o(c, block, DR, t_o):
    groups = [[0, 1], [2, 3], [4, 5], [6, 7]]
    @block.gpsimd
    def _(g):
        c.w(g, "scW")
        c.inc("cc2", g.collective_compute(
            "AllGather", ALU.bypass, replica_groups=groups,
            ins=[DR["ostore"].opt()], outs=[DR["og"].opt()]))
        g.wait_ge(c.S("cc2"), c.cnt["cc2"])
        if t_o is not None:
            c.inc("scW", g.dma_start(out=t_o[:, :], in_=DR["ostore"][:, :]), 16)


def emit_tail_a(c, block, I, DR, t_x1):
    nc = c.nc
    mk = c.mark()
    identb = c.sb["identbf"]
    sY = "taY"; sV = "taV"; sS = "taS"; sT = "taT"; sW = "taW"; sL = "taL"

    o_t = c.sbuf("ta_ot", [128, DV], F32)
    ssq = c.sbuf("ta_ssq", [128, 1], F32)
    rstd = c.sbuf("ta_rstd", [128, 1], F32)
    scr = c.sbuf("ta_scr", [128, DV], F32)
    onb = c.sbuf("ta_onb", [128, DV], F32)
    onrow = c.sbuf("ta_onrow", [1, DV], F32)
    gs_t = c.sbuf("ta_gst", [128, D], BF16)
    op_t = c.sbuf("ta_opt", [128, D], BF16)
    opT = c.sbuf("ta_opT", [16][0] if False else [128, D], BF16)
    wo_sb = [c.sbuf(f"ta_wo{i}", [128, D], BF16) for i in range(NCH)]
    x0_t = c.sbuf("ta_x0t", [128, D], F32)
    x1_t = c.sbuf("ta_x1t", [128, D], F32)
    pon = c.psum("ta_pon", [128, 512], F32)
    pot = c.psum("ta_potr", [128, 256], BF16)
    pwo = c.psum("ta_pwo", [128, D], F32)

    idx_tail = c.sbuf("idx_tail", [128, 1], mybir.dt.int32)
    @block.sync
    def _(s):
        c.w(s, "cc2")
        c.inc(sL, s.dma_start(out=onrow[0:1, :], in_=I["onorm_w"][None, :]), 16)
    @block.vector
    def _(v):
        c.w(v, sL)
        c.inc(sV, v.tensor_copy(c.sbuf("ta_onrow_bf", [1, DV], BF16)[:], onrow[0:1, :]))
    onrow_bf = c.sb["ta_onrow_bf"]
    @block.tensor
    def _(t):
        c.w(t, sV)
        c.inc(sT, t.matmul(pon[:, 0:128], c.sb["onesb_row"][0:1, 0:128], onrow_bf[0:1, :],
                           start=True, stop=True))
    @block.vector
    def _(v):
        c.w(v, sT)
        c.inc(sV, v.tensor_copy(onb[:], pon[:, 0:128]))

    for tt in range(NCH):
        @block.sync
        def _(s, tt=tt):
            c.w(s, sS)
            c.w(s, sL)
            c.w(s, sV)
            c.w(s, "cc2")
            c.inc(sL, s.dma_start(out=gs_t[:], in_=DR["gsig"][tt * 128:(tt + 1) * 128, :]), 16)
            c.inc(sL, s.dma_start(out=x0_t[:], in_=I["hs"][tt * 128:(tt + 1) * 128, :]), 16)

        for h in range(H):
            @block.gpsimd
            def _(g, tt=tt, h=h):
                c.w(g, sV)
                c.w(g, sL)
                c.w(g, sY)
                c.inc(sY, g.dma_start(out=idx_tail[:],
                                      in_=I["tail_idx"][tt * 16 + h:tt * 16 + h + 1, :].rearrange("o k -> k o"),
                                      allow_slow_non_contiguous=True), 16)
                g.wait_ge(c.S(sY), c.cnt[sY])
                c.inc(sY, g.indirect_dma_start(
                    o_t[:], None, DR["og"][:, :],
                    IndirectOffsetOnAxis(idx_tail[:, 0:1], 0)), 16)
            @block.scalar
            def _(s, tt=tt, h=h):
                c.w(s, sY)
                c.inc(sS, s.activation(scr[:], o_t[:], AF.Square, accum_out=ssq[:]))
            @block.vector
            def _(v, tt=tt, h=h):
                c.w(v, sS)
                c.inc(sV, v.tensor_scalar(ssq[:], ssq[:], 1.0 / DV, None, ALU.mult))
            @block.scalar
            def _(s, tt=tt, h=h):
                c.w(s, sV)
                c.inc(sS, s.activation(rstd[:], ssq[:], AF.Ln, bias=c.cc(1e-5)))
                c.w(s, sS)
                c.inc(sS, s.activation(rstd[:], rstd[:], AF.Exp, scale=c.cc(-0.5)))
            @block.vector
            def _(v, tt=tt, h=h):
                c.w(v, sS)
                c.w(v, sL)
                c.inc(sV, v.tensor_scalar_mul(scr[:], o_t[:], rstd[:]))
                c.w(v, sV)
                c.inc(sV, v.tensor_tensor(scr[:], scr[:], onb[:], ALU.mult))
                c.w(v, sV)
                c.inc(sV, v.tensor_tensor(op_t[:, h * DV:(h + 1) * DV], scr[:],
                                          gs_t[:, h * DV:(h + 1) * DV], ALU.mult))
        for ch in range(NCH):
            @block.tensor
            def _(t, tt=tt, ch=ch):
                c.w(t, sV)
                c.w(t, sS)
                c.inc(sT, t.transpose(pot[:, (ch % 2) * 128:(ch % 2) * 128 + 128],
                                      op_t[:, ch * 128:(ch + 1) * 128], identb[:]))
            @block.scalar
            def _(s, tt=tt, ch=ch):
                c.w(s, sT)
                c.inc(sS, s.copy(opT[:, ch * 128:(ch + 1) * 128],
                                 pot[:, (ch % 2) * 128:(ch % 2) * 128 + 128]))
        if tt == 0:
            @block.gpsimd
            def _(g):
                c.w(g, "fmW")
                for ch in range(NCH):
                    c.inc("taWl", g.dma_start(out=wo_sb[ch][:],
                                              in_=I["wo"][ch * 128:(ch + 1) * 128, :]), 16)
        @block.tensor
        def _(t, tt=tt):
            c.w(t, sS)
            c.w(t, "taWl")
            c.w(t, sV)
            for nt in range(4):
                for ch in range(NCH):
                    ins = t.matmul(pwo[:, nt * 512:(nt + 1) * 512], opT[:, ch * 128:(ch + 1) * 128],
                                   wo_sb[ch][:, nt * 512:(nt + 1) * 512],
                                   start=(ch == 0), stop=(ch == NCH - 1))
            c.inc(sT, ins)
        @block.vector
        def _(v, tt=tt):
            c.w(v, sT)
            c.w(v, sW)
            c.inc(sV, v.tensor_tensor(x1_t[:], pwo[:], x0_t[:], ALU.add))
        @block.sync
        def _(s, tt=tt):
            c.w(s, sV)
            c.inc(sW, s.dma_start(out=DR["x1s"][tt * 128:(tt + 1) * 128, :], in_=x1_t[:]), 16)
            if t_x1 is not None:
                c.inc(sW, s.dma_start(out=t_x1[tt * 128:(tt + 1) * 128, :], in_=x1_t[:]), 16)
    c.release(mk)


def emit_ffn(c, block, I, DR, xin, xout, tap_out, final=False, px="f1", dep=None):
    nc = c.nc
    mk = c.mark()
    identb = c.sb["identbf"]
    NE, FFB = 18, 11
    sL = px + "L"; sV = px + "V"; sS = px + "S"; sT = px + "T"; sW = px + "W"; sG = px + "G"
    sA = px + "A"

    emit_ln(c, block, xin, None, DR["z2s"], dep, None,
            first=False, prefix=px + "ln")

    zT = [c.sbuf(f"{px}_zT{i}", [128, NTOK], BF16) for i in range(NCH)]
    wpool = [c.sbuf(f"{px}_wp{i}", [128, D], BF16) for i in range(NCH)]
    w1sb = [wpool[i][:, 0:1408] for i in range(NCH)]
    w2sb = [wpool[i] for i in range(FFB)]
    b1c = c.sbuf(px + "_b1c", [128, NE * FFB], F32)
    b2cat = c.sbuf(px + "_b2cat", [18, D], BF16)
    gext = c.sbuf(px + "_gext", [18, NTOK], BF16)
    IDE = c.sbuf(px + "_IDE", [18, 18 * 128], BF16)
    centT = c.sbuf(px + "_centT", [128, NCH * 16], BF16)
    fb_b = c.sbuf(px + "_fbb", [128, 16], F32)
    sc_t = c.sbuf(px + "_sct", [128, 16], F32)
    work = c.sbuf(px + "_work", [128, 16], F32)
    mx = c.sbuf(px + "_mx", [128, 1], F32)
    gemask = c.sbuf(px + "_gemask", [128, 16], F32)
    gn = c.sbuf(px + "_gn", [128, 16], F32)
    sumg = c.sbuf(px + "_sumg", [128, 1], F32)
    hf_t = c.sbuf(px + "_hft", [128, 512], BF16)
    gbct = c.sbuf(px + "_gbct", [128, 512], BF16)
    htin = [c.sbuf(f"{px}_htin{i}", [128, FFB * 128], BF16) for i in range(2)]
    ysc = c.sbuf(px + "_ysc", [128, D], F32)
    xot = c.sbuf(px + "_xot", [128, D], F32)
    yld = c.sbuf(px + "_yld", [128, D], F32)
    z2tok = c.sbuf(px + "_z2tok", [128, D], BF16)
    ps_h = c.psum(px + "_psh", [128, 512], F32)
    ps_y = c.psum(px + "_psy", [128, D], F32)
    ps_g = c.psum(px + "_psg", [128, 512], F32)

    @block.vector
    def _(v):
        c.w(v, px + "lnS")
        c.w(v, px + "lnV")
        c.inc(sV, v.memset(ysc[:], 0.0))
        c.inc(sV, v.memset(IDE[:], 1.0))
        c.inc(sV, v.memset(gext[:, :], 1.0))
    @block.gpsimd
    def _(g):
        c.w(g, sV)
        c.w(g, px + "lnW")
        c.inc("initp", g.affine_select(IDE[:], IDE[:], pattern=[[-1, 18], [0, 128]],
                                       compare_op=ALU.is_equal, fill=0.0, base=0,
                                       channel_multiplier=1))
        for tt2 in range(NTOK // 128):
            c.inc(sG, g.dma_start(out=DR["ystore"][tt2 * 128:(tt2 + 1) * 128, :], in_=ysc[:]), 16)
        for e in range(NE):
            for f in range(FFB):
                src_ap = (I["f_rb1"][e:e + 1, f * 128:(f + 1) * 128] if e < 16 else
                          I["f_sb1"][e - 16:e - 15, f * 128:(f + 1) * 128])
                c.inc(sG, g.dma_start(out=b1c[:, e * FFB + f:e * FFB + f + 1],
                                      in_=src_ap.rearrange("o k -> k o"),
                                      allow_slow_non_contiguous=True), 16)
        for ch in range(NCH):
            c.inc(sG, g.dma_start(out=centT[:, ch * 16:(ch + 1) * 16],
                                  in_=I["f_cent"][:, ch * 128:(ch + 1) * 128].rearrange("e k -> k e"),
                                  allow_slow_non_contiguous=True), 16)
        c.inc(sG, g.dma_start(out=fb_b[:], in_=I["f_bias"][None, :].to_broadcast([128, 16])), 16)
        c.inc(sG, g.dma_start(out=b2cat[0:16, :], in_=I["f_rb2"][:, :]), 16)
        c.inc(sG, g.dma_start(out=b2cat[16:18, :], in_=I["f_sb2"][:, :]), 16)
    @block.scalar
    def _(s):
        c.w(s, px + "lnW")
        for ch in range(NCH):
            c.inc(sL, s.dma_start(out=zT[ch][:], in_=DR["z2s"][:, ch * 128:(ch + 1) * 128],
                                  transpose=True), 16)
    for tt in range(NTOK // 128):
        @block.tensor
        def _(t, tt=tt):
            c.w(t, sL)
            c.w(t, sG)
            c.w(t, sS)
            for ch in range(NCH):
                ins = t.matmul(ps_g[:, 0:16], zT[ch][:, tt * 128:(tt + 1) * 128],
                               centT[:, ch * 16:(ch + 1) * 16],
                               start=(ch == 0), stop=(ch == NCH - 1))
            c.inc(sT, ins)
        @block.scalar
        def _(s, tt=tt):
            c.w(s, sT)
            c.inc(sS, s.activation(sc_t[:], ps_g[:, 0:16], AF.Sigmoid))
        @block.vector
        def _(v, tt=tt):
            c.w(v, sS)
            c.w(v, sG)
            c.inc(sV, v.tensor_tensor(sc_t[:], sc_t[:], fb_b[:], ALU.add))
            c.w(v, sV)
            c.inc(sV, v.tensor_copy(work[:], sc_t[:]))
            for r in range(4):
                c.w(v, sV)
                c.inc(sV, v.tensor_reduce(mx[:], work[:], AX.X, ALU.max))
                if r < 3:
                    c.w(v, sV)
                    c.inc(sV, v.tensor_single_scalar(gemask[:], work[:], mx[:], ALU.is_ge))
                    c.w(v, sV)
                    c.inc(sV, v.scalar_tensor_tensor(work[:], gemask[:], -1e30, work[:],
                                                     ALU.mult, ALU.add))
            c.w(v, sV)
            c.inc(sV, v.tensor_single_scalar(gemask[:], sc_t[:], mx[:], ALU.is_ge))
            c.w(v, sV)
            c.inc(sV, v.tensor_tensor(gn[:], sc_t[:], gemask[:], ALU.mult))
            c.w(v, sV)
            c.inc(sV, v.tensor_reduce(sumg[:], gn[:], AX.X, ALU.add))
            c.w(v, sV)
            c.inc(sV, v.tensor_scalar_max(sumg[:], sumg[:], 1e-8))
            c.w(v, sV)
            c.inc(sV, v.reciprocal(sumg[:], sumg[:]))
            c.w(v, sV)
            c.inc(sV, v.tensor_scalar_mul(gn[:], gn[:], sumg[:]))
        @block.tensor
        def _(t, tt=tt):
            c.w(t, sV)
            c.inc(sT, t.transpose(ps_g[0:16, 0:128], gn[:], c.sb["identf32"][:]))
        @block.scalar
        def _(s, tt=tt):
            c.w(s, sT)
            c.inc(sS, s.copy(gext[0:16, tt * 128:(tt + 1) * 128], ps_g[0:16, 0:128]))

    for e in range(NE):
        @block.gpsimd
        def _(g, e=e):
            c.w(g, sS)
            c.w(g, sV)
            c.w(g, sT)
            c.w(g, sG)
            for ch in range(NCH):
                wsrc = (I["f_rw1"][e, ch * 128:(ch + 1) * 128, :] if e < 16 else
                        I["f_sw1"][e - 16, ch * 128:(ch + 1) * 128, :])
                c.inc(sG, g.dma_start(out=w1sb[ch], in_=wsrc), 16)
        for tta in range(4):
            @block.tensor
            def _(t, e=e, tta=tta):
                c.w(t, sG)
                c.w(t, sV)
                c.w(t, sS)
                c.inc(sT, t.matmul(ps_g[:], IDE[:, e * 128:(e + 1) * 128],
                                   gext[:, tta * 512:(tta + 1) * 512], start=True, stop=True))
            @block.vector
            def _(v, e=e, tta=tta):
                c.w(v, sT)
                c.inc(sV, v.tensor_copy(gbct[:], ps_g[:]))
            for f in range(FFB):
                @block.tensor
                def _(t, e=e, tta=tta, f=f):
                    c.w(t, sV)
                    c.w(t, sW)
                    for ch in range(NCH):
                        ins = t.matmul(ps_h[:], w1sb[ch][:, f * 128:(f + 1) * 128],
                                       zT[ch][:, tta * 512:(tta + 1) * 512],
                                       start=(ch == 0), stop=(ch == NCH - 1))
                    c.inc(sT, ins)
                @block.scalar
                def _(s, e=e, tta=tta, f=f):
                    c.w(s, sT)
                    c.w(s, sW)
                    c.inc(sS, s.activation(hf_t[:], ps_h[:], AF.Relu,
                                           bias=b1c[:, e * FFB + f:e * FFB + f + 1]))
                @block.vector
                def _(v, e=e, tta=tta, f=f):
                    c.w(v, sS)
                    if e < 16:
                        c.inc(sV, v.tensor_tensor(hf_t[:], hf_t[:], gbct[:], ALU.mult))
                @block.gpsimd
                def _(g, e=e, tta=tta, f=f):
                    c.w(g, sV)
                    c.w(g, sS)
                    c.inc(sW, g.dma_start(
                        out=DR["hstore"][(e * FFB + f) * 128:(e * FFB + f + 1) * 128,
                                         tta * 512:(tta + 1) * 512],
                        in_=hf_t[:]), 16)

    for e in range(NE):
        @block.gpsimd
        def _(g, e=e):
            c.w(g, sA)
            c.w(g, sV)
            c.w(g, sT)
            c.w(g, sW)
            for f in range(FFB):
                wsrc = (I["f_rw2"][e, f * 128:(f + 1) * 128, :] if e < 16 else
                        I["f_sw2"][e - 16, f * 128:(f + 1) * 128, :])
                c.inc(sG, g.dma_start(out=w2sb[f][:, :], in_=wsrc), 16)
        for tt in range(NTOK // 128):
            @block.scalar
            def _(s, e=e, tt=tt):
                c.w(s, sT)
                c.inc(sL, s.dma_start(
                    out=htin[tt % 2][:].rearrange("p (f t) -> p f t", t=128),
                    in_=DR["hstore"][e * FFB * 128:(e + 1) * FFB * 128,
                                     tt * 128:(tt + 1) * 128].rearrange("(f p) t -> p f t", p=128)), 16)
            @block.tensor
            def _(t, e=e, tt=tt):
                c.w(t, sL)
                c.w(t, sG)
                c.w(t, sV)
                for f in range(FFB):
                    for ntd in range(4):
                        ins = t.matmul(ps_y[:, ntd * 512:(ntd + 1) * 512],
                                       htin[tt % 2][:, f * 128:(f + 1) * 128],
                                       w2sb[f][:, ntd * 512:(ntd + 1) * 512],
                                       start=(f == 0), stop=(f == FFB - 1))
                c.inc(sT, ins)
            @block.vector
            def _(v, e=e, tt=tt):
                c.w(v, sT)
                c.w(v, sA)
                c.w(v, sG)
                c.inc(sV, v.tensor_copy(ysc[:], ps_y[:]))
            @block.gpsimd
            def _(g, e=e, tt=tt):
                c.w(g, sV)
                c.w(g, sG)
                c.inc(sA, g.dma_start(out=DR["ystore"][tt * 128:(tt + 1) * 128, :], in_=ysc[:],
                                      accum_op=ALU.add), 16)

    for tt in range(NTOK // 128):
        @block.sync
        def _(s, tt=tt):
            c.w(s, sA)
            c.w(s, sV)
            c.w(s, sL)
            c.inc(sL, s.dma_start(out=yld[:], in_=DR["ystore"][tt * 128:(tt + 1) * 128, :]), 16)
            c.inc(sL, s.dma_start(out=xot[:], in_=xin[tt * 128:(tt + 1) * 128, :]), 16)
            c.inc(sL, s.dma_start(out=z2tok[:], in_=DR["z2s"][tt * 128:(tt + 1) * 128, :]), 16)
        @block.tensor
        def _(t, tt=tt):
            c.w(t, sL)
            c.w(t, sV)
            for ntd in range(4):
                t.matmul(ps_y[:, ntd * 512:(ntd + 1) * 512], gext[:, tt * 128:(tt + 1) * 128],
                         b2cat[:, ntd * 512:(ntd + 1) * 512], start=True, stop=False)
                ins = t.matmul(ps_y[:, ntd * 512:(ntd + 1) * 512], identb[:],
                               z2tok[:, ntd * 512:(ntd + 1) * 512], start=False, stop=True)
            c.inc(sT, ins)
        @block.vector
        def _(v, tt=tt):
            c.w(v, sT)
            c.w(v, sW)
            c.inc(sV, v.tensor_tensor(ysc[:], ps_y[:], yld[:], ALU.add))
            c.w(v, sV)
            c.inc(sV, v.tensor_tensor(ysc[:], ysc[:], xot[:], ALU.add))
        @block.gpsimd
        def _(g, tt=tt):
            c.w(g, sV)
            c.inc(sW, g.dma_start(out=xout[tt * 128:(tt + 1) * 128, :], in_=ysc[:]), 16)
            if tap_out is not None:
                c.inc(sW, g.dma_start(out=tap_out[tt * 128:(tt + 1) * 128, :], in_=ysc[:]), 16)
    if final:
        @block.gpsimd
        def _(g):
            g.wait_ge(c.S(sW), c.cnt[sW])
    c.release(mk)
def emit_attn_b(c, block, I, DR, t_x3):
    nc = c.nc
    mk = c.mark()
    identb = c.sb["identbf"]
    ones_row = c.sb["onesb_row"]
    sL = "abL"; sV = "abV"; sS = "abS"; sT = "abT"; sG = "abG"; sY2 = "abY"
    sPW = "abPW"; sOW = "abOW"; sW = "abW"; sLk = "abLk"; sLq = "abLq"; sLo = "abLo"

    emit_ln(c, block, DR["x2s"], None, DR["z2s"], ["f1W"], None, first=False, prefix="abln")

    ps_s = c.psum("ab_pss", [128, 2048], F32)
    pTrb = c.psum("ab_ptrb", [128, 256], BF16)
    po = c.psum("ab_po", [128, 128], F32)

    @block.gpsimd
    def _(g):
        for name, dst in [("b_Wq_down", "wqd_b"), ("b_Wkv_down", "wkvd_b"),
                          ("b_Wq_up", "wqu_b"), ("b_Wk_up", "wku_b"),
                          ("b_Wv_up", "wvu_b"), ("b_Wo", "wo_b")]:
            c.inc(sG, g.dma_start(out=DR[dst][:, :], in_=I[name][:, :]), 16)

    mkp = c.mark()
    zTf = [c.sbuf(f"ab_zT{i}", [128, NTOK], BF16) for i in range(NCH)]
    wr = [c.sbuf(f"ab_wr{i}", [128, 512], BF16) for i in range(2)]
    lat_t = c.sbuf("ab_lat", [128, 512], BF16)
    latT = [c.sbuf(f"ab_latT{i}", [128, 128], BF16) for i in range(4)]
    upb = c.sbuf("ab_upb", [128, 2048], BF16)
    biasr = c.sbuf("ab_biasr", [1, D], BF16)
    posc = c.sbuf("ab_posc", [128, 1], F32)
    invf_b = c.sbuf("ab_invf", [128, 256], F32)
    frq = c.sbuf("ab_frq", [128, 256], F32)
    cosb = c.sbuf("ab_cos", [128, 256], F32)
    sinb = c.sbuf("ab_sin", [128, 256], F32)
    scr2 = c.sbuf("ab_scr2", [128, 256], F32)
    qfull = c.sbuf("ab_qfull", [128, 2048], BF16)

    @block.scalar
    def _(s):
        c.w(s, "ablnW")
        for ch in range(NCH):
            c.inc(sL, s.dma_start(out=zTf[ch][:], in_=DR["z2s"][:, ch * 128:(ch + 1) * 128],
                                  transpose=True), 16)
    @block.sync
    def _(s):
        c.inc(sY2, s.dma_start(out=invf_b[:], in_=I["invfreq"][None, :].to_broadcast([128, 256])), 16)

    for tt in range(NCH):
        @block.sync
        def _(s, tt=tt):
            c.w(s, sV)
            c.inc(sY2, s.dma_start(out=posc[:], in_=I["pos"][tt * 128:(tt + 1) * 128]
                                   .rearrange("(p o) -> p o", o=1)), 16)
        @block.vector
        def _(v, tt=tt):
            c.w(v, sY2)
            c.w(v, sS)
            c.inc(sV, v.tensor_scalar_mul(frq[:], invf_b[:], posc[:, 0:1]))
            for _r in range(1):
                c.w(v, sV)
                c.inc(sV, v.tensor_scalar_mul(scr2[:], frq[:], 0.15915494309644431))
                c.w(v, sV)
                c.inc(sV, v.tensor_scalar_add(scr2[:], scr2[:], 12582912.0))
                c.w(v, sV)
                c.inc(sV, v.tensor_scalar_add(scr2[:], scr2[:], -12582912.0))
                c.w(v, sV)
                c.inc(sV, v.scalar_tensor_tensor(frq[:], scr2[:], -6.283185307179586, frq[:],
                                                 ALU.mult, ALU.add))
        @block.scalar
        def _(s, tt=tt):
            c.w(s, sV)
            c.inc(sS, s.activation(sinb[:], frq[:], AF.Sin))
        @block.vector
        def _(v, tt=tt):
            c.w(v, sS)
            c.inc(sV, v.tensor_scalar_add(frq[:], frq[:], 1.5707963267948966))
            for _r in range(1):
                c.w(v, sV)
                c.inc(sV, v.tensor_scalar_mul(scr2[:], frq[:], 0.15915494309644431))
                c.w(v, sV)
                c.inc(sV, v.tensor_scalar_add(scr2[:], scr2[:], 12582912.0))
                c.w(v, sV)
                c.inc(sV, v.tensor_scalar_add(scr2[:], scr2[:], -12582912.0))
                c.w(v, sV)
                c.inc(sV, v.scalar_tensor_tensor(frq[:], scr2[:], -6.283185307179586, frq[:],
                                                 ALU.mult, ALU.add))
        @block.scalar
        def _(s, tt=tt):
            c.w(s, sV)
            c.inc(sS, s.activation(cosb[:], frq[:], AF.Sin))

        for which in ("q", "k", "v"):
            wd = "wqd_b" if which == "q" else "wkvd_b"
            bd = "b_bq_down" if which == "q" else "b_bkv_down"
            wu = {"q": "wqu_b", "k": "wku_b", "v": "wvu_b"}[which]
            bu = {"q": "b_bq_up", "k": "b_bk_up", "v": "b_bv_up"}[which]
            updim = 2048 if which == "v" else 1024
            if which != "v":
                @block.gpsimd
                def _(g, bd=bd):
                    c.w(g, sT)
                    c.inc(sY2, g.dma_start(out=biasr[0:1, 0:512], in_=I[bd][None, :]), 16)
                mmc = []
                for ch in range(NCH):
                    @block.scalar
                    def _(s, ch=ch, wd=wd):
                        if ch >= 2:
                            c.w(s, sT, mmc[ch - 2])
                        else:
                            c.w(s, sT)
                        if ch == 0:
                            c.w(s, sG)
                        c.inc(sL, s.dma_start(out=wr[ch % 2][:],
                                              in_=DR[wd][0:512, ch * 128:(ch + 1) * 128],
                                              transpose=True), 16)
                    @block.tensor
                    def _(t, ch=ch, tt=tt):
                        c.w(t, sL)
                        if ch == 0:
                            c.w(t, sS)
                        c.inc(sT, t.matmul(ps_s[:, 0:512],
                                           zTf[ch][:, tt * 128:(tt + 1) * 128],
                                           wr[ch % 2][:], start=(ch == 0), stop=False))
                    mmc.append(c.cnt[sT])
                @block.tensor
                def _(t):
                    c.w(t, sY2)
                    c.inc(sT, t.matmul(ps_s[:, 0:512], ones_row[0:1, 0:128],
                                       biasr[0:1, 0:512], start=False, stop=True))
                @block.scalar
                def _(s):
                    c.w(s, sT)
                    c.inc(sS, s.copy(lat_t[:], ps_s[:, 0:512]))
                for lc in range(4):
                    @block.tensor
                    def _(t, lc=lc):
                        c.w(t, sS)
                        c.inc(sT, t.transpose(pTrb[:, (lc % 2) * 128:(lc % 2 + 1) * 128],
                                              lat_t[:, lc * 128:(lc + 1) * 128], identb[:]))
                    @block.scalar
                    def _(s, lc=lc):
                        c.w(s, sT)
                        c.inc(sS, s.copy(latT[lc][:], pTrb[:, (lc % 2) * 128:(lc % 2 + 1) * 128]))
            @block.gpsimd
            def _(g, bu=bu, updim=updim):
                c.w(g, sT)
                c.inc(sY2, g.dma_start(out=biasr[0:1, 0:updim], in_=I[bu][None, :]), 16)
            mmc = []
            k = 0
            for nu in range(updim // 512):
                for lc in range(4):
                    @block.scalar
                    def _(s, lc=lc, nu=nu, wu=wu, k=k):
                        if k >= 2:
                            c.w(s, sT, mmc[k - 2])
                        else:
                            c.w(s, sT)
                        c.inc(sL, s.dma_start(out=wr[k % 2][:],
                                              in_=DR[wu][nu * 512:(nu + 1) * 512,
                                                         lc * 128:(lc + 1) * 128],
                                              transpose=True), 16)
                    @block.tensor
                    def _(t, lc=lc, nu=nu, k=k):
                        c.w(t, sL)
                        if k == 0:
                            c.w(t, sS)
                        c.inc(sT, t.matmul(ps_s[:, nu * 512:(nu + 1) * 512], latT[lc][:],
                                           wr[k % 2][:], start=(lc == 0), stop=False))
                    mmc.append(c.cnt[sT])
                    k += 1
            @block.tensor
            def _(t, updim=updim):
                c.w(t, sY2)
                for nu in range(updim // 512):
                    ins = t.matmul(ps_s[:, nu * 512:(nu + 1) * 512], ones_row[0:1, 0:128],
                                   biasr[0:1, nu * 512:(nu + 1) * 512], start=False, stop=True)
                c.inc(sT, ins)
            @block.scalar
            def _(s, updim=updim):
                c.w(s, sT)
                c.w(s, sV)
                c.w(s, sPW)
                c.inc(sS, s.copy(upb[:, 0:updim], ps_s[:, 0:updim]))
            if which == "v":
                @block.gpsimd
                def _(g, tt=tt):
                    c.w(g, sS)
                    c.inc(sPW, g.dma_start(out=DR["vb_d"][tt * 128:(tt + 1) * 128, :],
                                           in_=upb[:, 0:2048]), 16)
            else:
                @block.vector
                def _(v, tt=tt, which=which):
                    c.w(v, sS)
                    c.w(v, sPW)
                    c.inc(sV, v.tensor_copy(qfull[:, 0:1024], upb[:, 0:1024]))
                    for half, base in ((0, 1024), (1, 1536)):
                        for pi in range(2):
                            a = upb[:, pi * 256:(pi + 1) * 256]
                            b2 = upb[:, 512 + pi * 256:512 + (pi + 1) * 256]
                            dst = qfull[:, base + pi * 256:base + (pi + 1) * 256]
                            if half == 0:
                                c.inc(sV, v.tensor_tensor(scr2[:], a, cosb[:], ALU.mult))
                                c.w(v, sV)
                                c.inc(sV, v.tensor_tensor(frq[:], b2, sinb[:], ALU.mult))
                                c.w(v, sV)
                                c.inc(sV, v.tensor_tensor(dst, scr2[:], frq[:], ALU.subtract))
                            else:
                                c.inc(sV, v.tensor_tensor(scr2[:], a, sinb[:], ALU.mult))
                                c.w(v, sV)
                                c.inc(sV, v.tensor_tensor(frq[:], b2, cosb[:], ALU.mult))
                                c.w(v, sV)
                                c.inc(sV, v.tensor_tensor(dst, scr2[:], frq[:], ALU.add))
                @block.gpsimd
                def _(g, tt=tt, which=which):
                    c.w(g, sV)
                    dst = DR["qb_d"] if which == "q" else DR["kb_d"]
                    c.inc(sPW, g.dma_start(out=dst[tt * 128:(tt + 1) * 128, :], in_=qfull[:]), 16)
    c.release(mkp)

    @block.gpsimd
    def _(g):
        c.w(g, sPW)
        for name, gname in [("kb_d", "kbg"), ("vb_d", "vbg")]:
            c.inc("cc3", g.collective_compute(
                "AllGather", ALU.bypass, replica_groups=[[0, 1], [2, 3], [4, 5], [6, 7]],
                ins=[DR[name].opt()], outs=[DR[gname].opt()]))
        g.wait_ge(c.S("cc3"), c.cnt["cc3"])

    mkf = c.mark()
    kfm = c.sbuf("ab_kfm", [128, T], BF16)
    vh_sb = c.sbuf("ab_vh", [128, T], BF16)
    ssb = c.sbuf("ab_ssb", [128, T], F32)
    probs = c.sbuf("ab_probs", [128, T], BF16)
    pT_sb = c.sbuf("ab_pT", [128, T], BF16)
    qfmT = c.sbuf("ab_qfmT", [128, 128], BF16)
    negm = c.sbuf("ab_negm", [128, 1], F32)
    sume = c.sbuf("ab_sume", [128, 1], F32)
    invs = c.sbuf("ab_invs", [128, 1], F32)
    ob_t = c.sbuf("ab_obt", [128, 128], BF16)
    for h in range(H):
        @block.sync
        def _(s, h=h):
            c.w(s, "cc3")
            c.w(s, sT)
            c.inc(sLk, s.dma_start(out=vh_sb[:].rearrange("p (kt v) -> p kt v", v=128),
                                   in_=DR["vbg"][:, h * 128:(h + 1) * 128]
                                   .rearrange("(kt p) v -> p kt v", p=128)), 16)
            for kt in range(8):
                c.inc(sLk, s.dma_start(out=kfm[:, kt * 512:(kt + 1) * 512],
                                       in_=DR["kbg"][kt * 512:(kt + 1) * 512,
                                                     h * 128:(h + 1) * 128],
                                       transpose=True), 16)
        for tt in range(NCH):
            @block.scalar
            def _(s, h=h, tt=tt):
                c.w(s, sT)
                c.inc(sLq, s.dma_start(out=qfmT[:],
                                       in_=DR["qb_d"][tt * 128:(tt + 1) * 128,
                                                      h * 128:(h + 1) * 128],
                                       transpose=True), 16)
            @block.vector
            def _(v, h=h, tt=tt):
                c.w(v, sLq)
                c.inc(sV, v.tensor_single_scalar(qfmT[:], qfmT[:], 0.0883883476483184,
                                                 ALU.mult))
            for half in range(2):
                @block.tensor
                def _(t, h=h, tt=tt, half=half):
                    c.w(t, sV)
                    c.w(t, sLk)
                    c.w(t, sS)
                    for kq in range(4):
                        kt = half * 4 + kq
                        c.inc(sT, t.matmul(ps_s[:, kq * 512:(kq + 1) * 512], qfmT[:],
                                           kfm[:, kt * 512:(kt + 1) * 512],
                                           start=True, stop=True))
                @block.scalar
                def _(s, h=h, tt=tt, half=half):
                    c.w(s, sT)
                    c.w(s, sV)
                    c.inc(sS, s.copy(ssb[:, half * 2048:(half + 1) * 2048], ps_s[:, 0:2048]))
            @block.vector
            def _(v, h=h, tt=tt):
                c.w(v, sS)
                c.inc(sV, v.tensor_reduce(negm[:], ssb[:], AX.X, ALU.max, negate=True))
            @block.scalar
            def _(s, h=h, tt=tt):
                c.w(s, sV)
                c.w(s, "abPT")
                c.inc(sS, s.activation(probs[:], ssb[:], AF.Exp, bias=negm[:],
                                       accum_out=sume[:]))
            @block.vector
            def _(v, h=h, tt=tt):
                c.w(v, sS)
                c.inc(sV, v.reciprocal(invs[:], sume[:]))
            evc = []
            for kt in range(32):
                @block.tensor
                def _(t, kt=kt):
                    c.w(t, sS)
                    if kt >= 2:
                        c.w(t, "abPT", evc[kt - 2])
                    c.inc(sT, t.transpose(pTrb[:, (kt % 2) * 128:(kt % 2 + 1) * 128],
                                          probs[:, kt * 128:(kt + 1) * 128], identb[:]))
                @block.scalar
                def _(s, kt=kt):
                    c.w(s, sT)
                    c.inc("abPT", s.copy(pT_sb[:, kt * 128:(kt + 1) * 128],
                                         pTrb[:, (kt % 2) * 128:(kt % 2 + 1) * 128]))
                evc.append(c.cnt["abPT"])
            @block.tensor
            def _(t, h=h, tt=tt):
                c.w(t, "abPT")
                c.w(t, sV)
                for kt in range(32):
                    ins = t.matmul(po[:], pT_sb[:, kt * 128:(kt + 1) * 128],
                                   vh_sb[:, kt * 128:(kt + 1) * 128],
                                   start=(kt == 0), stop=(kt == 31))
                c.inc(sT, ins)
            @block.vector
            def _(v, h=h, tt=tt):
                c.w(v, sT)
                c.w(v, sV)
                c.w(v, sOW)
                c.inc(sV, v.tensor_scalar_mul(ob_t[:], po[:], invs[:, 0:1]))
            @block.gpsimd
            def _(g, h=h, tt=tt):
                c.w(g, sV)
                c.inc(sOW, g.dma_start(out=DR["obd"][tt * 128:(tt + 1) * 128,
                                                     h * 128:(h + 1) * 128],
                                       in_=ob_t[:]), 16)
    c.release(mkf)

    mko = c.mark()
    obT = [c.sbuf(f"ab_obT{i}", [128, 128], BF16) for i in range(NCH)]
    wo_sb = [c.sbuf(f"ab_wo{i}", [128, D], BF16) for i in range(NCH)]
    biasr2 = c.sbuf("ab_biasr2", [1, D], BF16)
    x2t = c.sbuf("ab_x2t", [128, D], F32)
    x3t = c.sbuf("ab_x3t", [128, D], F32)
    @block.gpsimd
    def _(g):
        c.inc(sY2, g.dma_start(out=biasr2[0:1, :], in_=I["b_bo"][None, :]), 16)
    @block.sync
    def _(s):
        c.w(s, sOW)
        for ch in range(NCH):
            for ns in range(4):
                c.inc(sLo, s.dma_start(out=wo_sb[ch][:, ns * 512:(ns + 1) * 512],
                                       in_=DR["wo_b"][ns * 512:(ns + 1) * 512,
                                                      ch * 128:(ch + 1) * 128],
                                       transpose=True), 16)
    for tt in range(NCH):
        @block.scalar
        def _(s, tt=tt):
            c.w(s, sT)
            for ch in range(NCH):
                c.inc(sLq, s.dma_start(out=obT[ch][:],
                                       in_=DR["obd"][tt * 128:(tt + 1) * 128,
                                                     ch * 128:(ch + 1) * 128],
                                       transpose=True), 16)
        @block.sync
        def _(s, tt=tt):
            c.w(s, sV)
            c.inc(sY2, s.dma_start(out=x2t[:], in_=DR["x2s"][tt * 128:(tt + 1) * 128, :]), 16)
        @block.tensor
        def _(t, tt=tt):
            c.w(t, sLq)
            c.w(t, sLo)
            c.w(t, sY2)
            c.w(t, sV)
            for ns in range(4):
                for ch in range(NCH):
                    t.matmul(ps_s[:, ns * 512:(ns + 1) * 512], obT[ch][:],
                             wo_sb[ch][:, ns * 512:(ns + 1) * 512],
                             start=(ch == 0), stop=False)
                ins = t.matmul(ps_s[:, ns * 512:(ns + 1) * 512], ones_row[0:1, 0:128],
                               biasr2[0:1, ns * 512:(ns + 1) * 512], start=False, stop=True)
            c.inc(sT, ins)
        @block.vector
        def _(v, tt=tt):
            c.w(v, sT)
            c.w(v, sW)
            c.inc(sV, v.tensor_tensor(x3t[:], ps_s[:, :], x2t[:], ALU.add))
        @block.gpsimd
        def _(g, tt=tt):
            c.w(g, sV)
            c.inc(sW, g.dma_start(out=DR["x3s"][tt * 128:(tt + 1) * 128, :], in_=x3t[:]), 16)
            if t_x3 is not None:
                c.inc(sW, g.dma_start(out=t_x3[tt * 128:(tt + 1) * 128, :], in_=x3t[:]), 16)
    c.release(mko)
    c.release(mk)


def make_in_maps(inputs):
    hs = np.ascontiguousarray(inputs["L0_attn_a_hidden_states"], np.float32)
    ren = {
        "wq": "L0_attn_a_wq", "wk": "L0_attn_a_wk", "wv": "L0_attn_a_wv",
        "conv_q_w": "L0_attn_a_conv_q_w", "conv_k_w": "L0_attn_a_conv_k_w",
        "conv_v_w": "L0_attn_a_conv_v_w", "A_log": "L0_attn_a_A_log",
        "dt_bias": "L0_attn_a_dt_bias", "proxy_w": "L0_attn_a_proxy_w",
        "unc_w": "L0_attn_a_unc_w", "head_embed": "L0_attn_a_head_embed",
        "mlp_w1": "L0_attn_a_mlp_w1", "mlp_b1": "L0_attn_a_mlp_b1",
        "mlp_w2": "L0_attn_a_mlp_w2", "mlp_b2": "L0_attn_a_mlp_b2",
        "bb_w": "L0_attn_a_bb_w", "bb_b": "L0_attn_a_bb_b",
        "bd_w": "L0_attn_a_bd_w", "bd_b": "L0_attn_a_bd_b",
        "lam_w": "L0_attn_a_lam_w", "lam_b": "L0_attn_a_lam_b",
        "ab_w": "L0_attn_a_ab_w", "ab_b": "L0_attn_a_ab_b",
        "ad_w": "L0_attn_a_ad_w", "ad_b": "L0_attn_a_ad_b",
        "g1_w": "L0_attn_a_g1_w", "g2_w": "L0_attn_a_g2_w", "g2_b": "L0_attn_a_g2_b",
        "onorm_w": "L0_attn_a_onorm_w", "wo": "L0_attn_a_wo",
        "b_Wq_down": "L0_attn_b_Wq_down", "b_bq_down": "L0_attn_b_bq_down",
        "b_Wkv_down": "L0_attn_b_Wkv_down", "b_bkv_down": "L0_attn_b_bkv_down",
        "b_Wq_up": "L0_attn_b_Wq_up", "b_bq_up": "L0_attn_b_bq_up",
        "b_Wk_up": "L0_attn_b_Wk_up", "b_bk_up": "L0_attn_b_bk_up",
        "b_Wv_up": "L0_attn_b_Wv_up", "b_bv_up": "L0_attn_b_bv_up",
        "b_Wo": "L0_attn_b_Wo", "b_bo": "L0_attn_b_bo",
        "f_sw1": "L0_ffn_shared_w1", "f_sb1": "L0_ffn_shared_b1",
        "f_sw2": "L0_ffn_shared_w2", "f_sb2": "L0_ffn_shared_b2",
        "f_rw1": "L0_ffn_routed_w1", "f_rb1": "L0_ffn_routed_b1",
        "f_rw2": "L0_ffn_routed_w2", "f_rb2": "L0_ffn_routed_b2",
        "f_cent": "L0_ffn_centroids", "f_bias": "L0_ffn_bias",
    }
    base = {k: np.ascontiguousarray(inputs[v], np.float32) for k, v in ren.items()}
    def level_mask(l):
        m = np.zeros((128, 128), np.float32)
        for b in range(128 // (2 * l)):
            m[2 * l * b:2 * l * b + l, 2 * l * b + l:2 * l * b + 2 * l] = 1.0
        return m
    base["bigmask"] = level_mask(64) + level_mask(32) + level_mask(16)
    base["mask4"] = np.concatenate([level_mask(l) for l in (8, 4, 2, 1)], axis=1)
    TT = 4096
    maps = []
    for core in range(8):
        b, hf = core // 2, core % 2
        lo = hf * NTOK
        m = dict(base)
        m["hs"] = np.ascontiguousarray(hs[b, lo:lo + NTOK, :])
        hb = hf * 8
        kqv = np.zeros((8, 256), np.int32)
        for j in range(8):
            for half in range(2):
                kqv[j, half * 128:(half + 1) * 128] = half * 2048 + (hb + j) * 128 + np.arange(128)
        m["kqv_idx"] = kqv
        sri = np.zeros(80, np.int32)
        for s5 in range(5):
            for half in range(2):
                for j in range(8):
                    sri[(s5 * 2 + half) * 8 + j] = 80 * half + 16 * s5 + hb + j
        m["srow_idx"] = sri[None, :]
        m["gch_idx"] = (hb + np.arange(8, dtype=np.int32))[None, :]
        ai = np.zeros(32, np.int32)
        for sp in range(2):
            for half in range(2):
                for j in range(8):
                    ai[(sp * 2 + half) * 8 + j] = 80 * half + 48 + 16 * sp + hb + j
        m["amp_idx"] = ai[None, :]
        ti = np.zeros((256, 128), np.int32)
        for tt in range(16):
            for h in range(16):
                ti[tt * 16 + h, :] = (h // 8) * 8 * 4096 + (h % 8) * 4096 + hf * NTOK + tt * 128 + np.arange(128)
        m["tail_idx"] = ti
        m["pos"] = (hf * NTOK + np.arange(NTOK)).astype(np.float32)
        m["invfreq"] = (1.0 / 10000.0 ** (np.arange(0, 512, 2, dtype=np.float32) / 512.0))
        if hf == 0:
            m["hs_bnd"] = np.zeros((3, D), np.float32)
        else:
            m["hs_bnd"] = np.ascontiguousarray(hs[b, lo - 3:lo, :])
        maps.append(m)
    return maps


def kernel(**inputs):
    nc, tap_names = build_nc()
    in_maps = make_in_maps(inputs)
    res = run_bass_kernel_spmd(nc, in_maps, core_ids=list(range(8)))
    out = np.zeros((B, T, D), np.float32)
    for core in range(8):
        b, hf = core // 2, core % 2
        out[b, hf * NTOK:(hf + 1) * NTOK, :] = res.results[core]["out"]
    kernel._last_results = res.results
    kernel._tap_names = tap_names
    return out
